# version 04
# baseline (speedup 1.0000x reference)
import math
from contextlib import ExitStack

import numpy as np
import ml_dtypes
import concourse.bass as bass
import concourse.mybir as mybir
from concourse.bass_utils import run_bass_kernel_spmd

bf16 = ml_dtypes.bfloat16
F32 = mybir.dt.float32
BF16 = mybir.dt.bfloat16
AF = mybir.ActivationFunctionType
ALU = mybir.AluOpType
AX = mybir.AxisListType

F32EPS = float(np.finfo(np.float32).eps)
ENGS = ("sync", "vector", "scalar", "tensor", "gpsimd")


class Cfg:
    def __init__(self, TB=8192, C=2048, H=16, G=4, U=8192):
        self.S = 4
        self.TB = TB
        self.TO = TB // self.S
        self.C = C
        self.H = H
        self.G = G
        self.KV = G * 128
        self.U = U
        self.QT = 512
        self.KB = 128
        self.CW = self.KB // self.S
        self.QG = 64
        self.NCC = C // 128
        self.NM = self.TO // self.QT
        self.SPAN = self.S * self.QT // self.KB
        self.NU = U // 128
        self.NCO = C // 512
        self.NTT = self.TO // 128
        assert self.H * 128 == C and self.TO % self.QT == 0
        assert C % 512 == 0 and U % 512 == 0 and TB % 512 == 0
        assert self.SPAN % 2 == 0


class Sem:
    def __init__(self, handle):
        self.h = handle
        self.n = 0

    def inc(self, d=1):
        self.n += d
        return self.n


class Prog:
    def __init__(self):
        self.q = {k: [] for k in ENGS}

    def add(self, eng, fn):
        self.q[eng].append(fn)

    def wait(self, eng, sem, val):
        if val > 0:
            f = lambda E, s=sem.h, v=val: E.wait_ge(s, v)
            f.is_wait = True
            self.q[eng].append(f)

    def seal_tensor_inc(self, sem):
        f = self.q["tensor"][-1]
        v = sem.inc()
        self.q["tensor"][-1] = lambda E, f=f, s=sem.h: f(E).then_inc(s, 1)
        return v

    def seal_vector_inc(self, sem):
        f = self.q["vector"][-1]
        v = sem.inc()
        self.q["vector"][-1] = lambda E, f=f, s=sem.h: f(E).then_inc(s, 1)
        return v

    def emit(self, nc):
        with nc.Block() as block:
            for eng in ENGS:
                ops = self.q[eng]
                if not ops or all(getattr(f, "is_wait", False) for f in ops):
                    continue

                def body(E, ops=ops):
                    for f in ops:
                        f(E)

                getattr(block, eng)(body)


def build_nc(cfg: Cfg, dbg=False, stages=9, abl=()):
    c = cfg
    nc = bass.Bass("TRN2", target_bir_lowering=False, detect_race_conditions=False)

    x_b = nc.dram_tensor("x_b", [c.TB, c.C], F32, kind="ExternalInput")
    x_o = nc.dram_tensor("x_o", [c.TO, c.C], F32, kind="ExternalInput")
    wqT = nc.dram_tensor("wqT", [c.C, c.C], F32, kind="ExternalInput")
    wkT = nc.dram_tensor("wkT", [c.C, c.KV], F32, kind="ExternalInput")
    wvT = nc.dram_tensor("wvT", [c.C, c.KV], F32, kind="ExternalInput")
    woT = nc.dram_tensor("woT", [c.C, c.C], F32, kind="ExternalInput")
    bo_t = nc.dram_tensor("bo", [1, c.C], F32, kind="ExternalInput")
    qnw_t = nc.dram_tensor("qnw", [128, 1], F32, kind="ExternalInput")
    knw_t = nc.dram_tensor("knw", [128, 1], F32, kind="ExternalInput")
    cos_b = nc.dram_tensor("cos_b", [128, c.TB], F32, kind="ExternalInput")
    sin_b = nc.dram_tensor("sin_b", [128, c.TB], F32, kind="ExternalInput")
    cos_o = nc.dram_tensor("cos_o", [128, c.TO], F32, kind="ExternalInput")
    sin_o = nc.dram_tensor("sin_o", [128, c.TO], F32, kind="ExternalInput")
    mask_t = nc.dram_tensor("mask", [128, c.CW], F32, kind="ExternalInput")
    swp_t = nc.dram_tensor("swp", [128, 128], F32, kind="ExternalInput")
    wgu = nc.dram_tensor("wgu", [2 * c.U, c.C], F32, kind="ExternalInput")
    wp = nc.dram_tensor("wp", [c.C, c.U], F32, kind="ExternalInput")
    out_t = nc.dram_tensor("out", [c.TO, c.C], F32, kind="ExternalOutput")

    kk = dict(kind="ExternalOutput") if dbg else {}
    ln1_b = nc.dram_tensor("ln1_b", [c.TB, c.C], BF16, **kk)
    ln1_o = nc.dram_tensor("ln1_o", [c.TO, c.C], BF16, **kk)
    ln2_o = nc.dram_tensor("ln2_o", [c.TO, c.C], BF16, **kk)
    x1_o = nc.dram_tensor("x1_o", [c.TO, c.C], F32, **kk)
    wgu_q = nc.dram_tensor("wgu_q", [2 * c.U, c.C], BF16, **kk)
    wp_q = nc.dram_tensor("wp_q", [c.C, c.U], BF16, **kk)
    wqT_h = nc.dram_tensor("wqT_h", [c.C, c.C], BF16, **kk)
    woT_h = nc.dram_tensor("woT_h", [c.C, c.C], BF16, **kk)
    h_sc = nc.dram_tensor("h_sc", [c.U, c.TO], BF16, **kk)
    y_sc = nc.dram_tensor("y_sc", [128, c.H * c.TO], BF16, **kk)
    inv_sc = nc.dram_tensor("inv_sc", [1, c.TO], F32, **kk)
    q_dbg = nc.dram_tensor("q_dbg", [128, 4096], F32, **kk) if dbg else None

    ctx = ExitStack()
    e = ctx.enter_context

    ones_b = e(nc.sbuf_tensor("ones_b", [128, 1], BF16))
    ones1 = e(nc.sbuf_tensor("ones1", [1, 128], F32))
    onesf = e(nc.sbuf_tensor("onesf", [128, 1], F32))
    qnw_s = e(nc.sbuf_tensor("qnw_s", [128, 1], F32))
    knw_s = e(nc.sbuf_tensor("knw_s", [128, 1], F32))
    mask_s = e(nc.sbuf_tensor("mask_s", [128, c.CW], F32))
    swp_s = e(nc.sbuf_tensor("swp_s", [128, 128], F32))
    lnb = {}
    eps_ln = e(nc.sbuf_tensor("eps_ln", [128, 1], F32))
    padt = e(nc.sbuf_tensor("padt", [128, 8], F32))
    padt2 = e(nc.sbuf_tensor("padt2", [128, 8], F32))
    invC = e(nc.sbuf_tensor("invC", [128, 1], F32))
    eps_ffn = e(nc.sbuf_tensor("eps_ffn", [128, 1], F32))


    d = [Sem(e(nc.semaphore(f"d{i}"))) for i in range(8)]
    sv = Sem(e(nc.semaphore("sv")))
    sp = Sem(e(nc.semaphore("sp")))
    sa = Sem(e(nc.semaphore("sa")))
    st = Sem(e(nc.semaphore("st")))
    ALLSEMS = tuple(d) + (sv, sp, sa, st)

    def phase_start(p: Prog):
        for eng in ENGS:
            for s in ALLSEMS:
                p.wait(eng, s, s.n)

    def vpad(p, n=5):
        for _ in range(n):
            p.add("vector", lambda E: E.tensor_copy(padt[:, :], padt2[:, :]))

    def dma(p, eng, sem, out, in_, transpose=False):
        if transpose:
            p.add(eng, lambda E, o=out, i=in_, s=sem.h:
                  E.dma_start_transpose(o, i).then_inc(s, 16))
        else:
            p.add(eng, lambda E, o=out, i=in_, s=sem.h:
                  E.dma_start(out=o, in_=i).then_inc(s, 16))
        return sem.inc(16)

    pQ = Prog()
    qctx = ExitStack()
    eq = qctx.enter_context
    QW = 2048
    qtiles = []
    for r0 in range(0, 2 * c.U, 128):
        for c0 in range(0, c.C, QW):
            qtiles.append((wgu, wgu_q, r0, c0, min(QW, c.C - c0)))
    for r0 in range(0, c.C, 128):
        for c0 in range(0, c.U, QW):
            qtiles.append((wp, wp_q, r0, c0, min(QW, c.U - c0)))

    q_win = [eq(nc.sbuf_tensor(f"q_win{i}", [128, QW], F32)) for i in range(2)]
    q_wb = eq(nc.sbuf_tensor("q_wb", [128, QW], BF16))
    q_sum = eq(nc.sbuf_tensor("q_sum", [128, QW // c.QG], F32))
    q_scl = eq(nc.sbuf_tensor("q_scl", [128, QW // c.QG], BF16))
    q_rcp = eq(nc.sbuf_tensor("q_rcp", [128, QW // c.QG], F32))
    q_sclf = eq(nc.sbuf_tensor("q_sclf", [128, QW // c.QG], F32))
    q_qq = eq(nc.sbuf_tensor("q_qq", [128, QW], BF16))
    q_wh = eq(nc.sbuf_tensor("q_wh", [128, QW], BF16))
    q_out = [eq(nc.sbuf_tensor(f"q_out{i}", [128, QW], BF16)) for i in range(2)]
    c_wq = [eq(nc.sbuf_tensor(f"c_wq{i}", [128, c.C], BF16)) for i in range(2)]

    dma(pQ, "sync", d[6], qnw_s[:, :], qnw_t[:, :])
    dma(pQ, "sync", d[6], knw_s[:, :], knw_t[:, :])
    dma(pQ, "sync", d[6], mask_s[:, :], mask_t[:, :])
    dma(pQ, "sync", d[6], swp_s[:, :], swp_t[:, :])
    pQ.add("vector", lambda E: E.memset(ones_b[:, :], 1.0))
    pQ.add("vector", lambda E: E.memset(ones1[:, :], 1.0))
    pQ.add("vector", lambda E: E.memset(onesf[:, :], 1.0))
    pQ.add("vector", lambda E: E.memset(eps_ln[:, :], 1e-5))
    pQ.add("vector", lambda E: E.memset(eps_ffn[:, :], F32EPS))
    pQ.add("vector", lambda E: E.memset(invC[:, :], 1.0 / c.C))
    pQ.add("vector", lambda E: E.memset(padt2[:, :], 0.0))

    q_winfree, q_outrdy, q_outdma = [], [], []
    for i, (srct, dstt, r0, c0, w) in enumerate(qtiles):
        g = w // c.QG
        if i >= 2:
            pQ.wait("sync", sv, q_winfree[i - 2])
        qin = dma(pQ, "sync", d[i % 2], q_win[i % 2][:, 0:w], srct[r0:r0 + 128, c0:c0 + w])

        pQ.wait("vector", d[i % 2], qin)
        if i >= 2:
            pQ.wait("vector", d[2 + (i % 2)], q_outdma[i - 2])
        wi, ob = q_win[i % 2], q_out[i % 2]
        pQ.add("vector", lambda E, wi=wi, w=w: E.tensor_copy(q_wb[:, 0:w], wi[:, 0:w]))
        q_winfree.append(pQ.seal_vector_inc(sv))
        pQ.wait("vector", sv, q_winfree[i])
        pQ.add("vector", lambda E, w=w, g=g: E.tensor_reduce(
            q_sum[:, 0:g], q_wb[:, 0:w].rearrange("p (g x) -> p g x", x=c.QG),
            AX.X, ALU.add, apply_absolute_value=True))
        vpad(pQ)
        pQ.add("vector", lambda E, g=g: E.tensor_scalar(
            q_sclf[:, 0:g], q_sum[:, 0:g], 1.0 / c.QG, 1e-8, ALU.mult, ALU.max))
        vpad(pQ)
        pQ.add("vector", lambda E, g=g: E.tensor_copy(q_scl[:, 0:g], q_sclf[:, 0:g]))
        vpad(pQ)
        pQ.add("vector", lambda E, g=g: E.tensor_copy(q_sclf[:, 0:g], q_scl[:, 0:g]))
        vpad(pQ)
        pQ.add("vector", lambda E, g=g: E.reciprocal(q_rcp[:, 0:g], q_sclf[:, 0:g]))
        vpad(pQ)
        pQ.add("vector", lambda E, w=w, g=g: E.tensor_tensor(
            q_qq[:, 0:w].rearrange("p (g x) -> p g x", x=c.QG),
            q_wb[:, 0:w].rearrange("p (g x) -> p g x", x=c.QG),
            q_rcp[:, 0:g].broadcast_to([128, g, c.QG]), ALU.mult))
        pQ.add("vector", lambda E, w=w: E.tensor_scalar(
            q_qq[:, 0:w], q_qq[:, 0:w], 192.0, None, ALU.add))
        pQ.add("vector", lambda E, w=w: E.tensor_scalar(
            q_qq[:, 0:w], q_qq[:, 0:w], 192.0, 1.0, ALU.subtract, ALU.min))
        pQ.add("vector", lambda E, w=w: E.tensor_scalar(
            q_qq[:, 0:w], q_qq[:, 0:w], -1.0, None, ALU.max))
        pQ.add("vector", lambda E, w=w, g=g: E.tensor_tensor(
            q_wh[:, 0:w].rearrange("p (g x) -> p g x", x=c.QG),
            q_qq[:, 0:w].rearrange("p (g x) -> p g x", x=c.QG),
            q_sclf[:, 0:g].broadcast_to([128, g, c.QG]), ALU.mult))
        pQ.add("vector", lambda E, w=w: E.tensor_sub(q_wh[:, 0:w], q_wh[:, 0:w], q_wb[:, 0:w]))
        pQ.add("vector", lambda E, ob=ob, w=w: E.tensor_add(
            ob[:, 0:w], q_wb[:, 0:w], q_wh[:, 0:w]))
        q_outrdy.append(pQ.seal_vector_inc(sv))

        pQ.wait("scalar", sv, q_outrdy[i])
        q_outdma.append(dma(pQ, "scalar", d[2 + (i % 2)], dstt[r0:r0 + 128, c0:c0 + w], ob[:, 0:w]))

    ctiles = [(wqT, wqT_h, r0) for r0 in range(0, c.C, 128)] + \
             [(woT, woT_h, r0) for r0 in range(0, c.C, 128)]
    cw_store = []
    for i, (srct, dstt, r0) in enumerate(ctiles):
        ds = d[4 + (i % 2)]
        if i >= 2:
            pQ.wait("gpsimd", ds, cw_store[i - 2])
        v = dma(pQ, "gpsimd", ds, c_wq[i % 2][:, :], srct[r0:r0 + 128, :])
        pQ.wait("gpsimd", ds, v)
        cw_store.append(dma(pQ, "gpsimd", ds, dstt[r0:r0 + 128, :], c_wq[i % 2][:, :]))
    pQ.emit(nc)
    qctx.close()

    p1 = Prog()
    phase_start(p1)
    c1 = ExitStack()
    e1 = c1.enter_context
    lnb["scr"] = e1(nc.sbuf_tensor("l_scr1", [128, c.C], F32))
    lnb["st"] = e1(nc.sbuf_tensor("l_st1", [128, 12], F32))
    l_x = [e1(nc.sbuf_tensor(f"l_x{i}", [128, c.C], F32)) for i in range(2)]
    l_o = [e1(nc.sbuf_tensor(f"l_o{i}", [128, c.C], BF16)) for i in range(2)]

    def ln_tile(p: Prog, xbuf, obuf, par, eps):
        o = 6 * par
        l_scr, l_st = lnb["scr"], lnb["st"]
        p.add("vector", lambda E: E.reduce_sum(
            l_st[:, o + 0:o + 1], xbuf[:, :], AX.X))
        p.add("vector", lambda E: E.tensor_mul(l_scr[:, :], xbuf[:, :], xbuf[:, :]))
        p.add("vector", lambda E: E.reduce_sum(
            l_st[:, o + 1:o + 2], l_scr[:, :], AX.X))
        vred = p.seal_vector_inc(sv)
        p.wait("gpsimd", sv, vred)
        p.add("gpsimd", lambda E: E.tensor_mul(
            l_st[:, o + 2:o + 3], l_st[:, o + 0:o + 1], invC[:, :]))
        p.add("gpsimd", lambda E: E.tensor_mul(
            l_st[:, o + 3:o + 4], l_st[:, o + 2:o + 3], l_st[:, o + 2:o + 3]))
        p.add("gpsimd", lambda E: E.tensor_mul(
            l_st[:, o + 4:o + 5], l_st[:, o + 1:o + 2], invC[:, :]))
        p.add("gpsimd", lambda E: E.tensor_sub(
            l_st[:, o + 4:o + 5], l_st[:, o + 4:o + 5], l_st[:, o + 3:o + 4])
            .then_inc(sp.h, 1))
        gva = sp.inc()
        p.wait("scalar", sp, gva)
        p.add("scalar", lambda E: E.activation(
            l_st[:, o + 5:o + 6], l_st[:, o + 4:o + 5], AF.Sqrt, bias=eps_ln[:, 0:1], scale=1.0)
            .then_inc(sa.h, 1))
        saa = sa.inc()
        p.wait("vector", sa, saa)
        vpad(p)
        p.add("vector", lambda E: E.reciprocal(l_st[:, o + 5:o + 6], l_st[:, o + 5:o + 6]))
        vpad(p, 6)
        p.add("vector", lambda E: E.tensor_scalar(
            obuf[:, :], xbuf[:, :], l_st[:, o + 2:o + 3], l_st[:, o + 5:o + 6],
            ALU.subtract, ALU.mult))
        return p.seal_vector_inc(sv)

    ln_xfree, ln_odma = [], []
    ln_srcs = [(x_b, ln1_b, c.TB // 128)] + [(x_o, ln1_o, c.TO // 128)]
    i = 0
    for src_dram, dst_dram, ntiles in ln_srcs:
        for j in range(ntiles):
            par = i % 2
            if i >= 2:
                p1.wait("sync", sv, ln_xfree[i - 2])
            lin = dma(p1, "sync", d[par], l_x[par][:, :], src_dram[j * 128:(j + 1) * 128, :])
            p1.wait("vector", d[par], lin)
            if i >= 2:
                p1.wait("vector", d[2 + par], ln_odma[i - 2])
            ln_xfree.append(ln_tile(p1, l_x[par], l_o[par], par, 1e-5))
            p1.wait("scalar", sv, ln_xfree[i])
            ln_odma.append(dma(p1, "scalar", d[2 + par],
                               dst_dram[j * 128:(j + 1) * 128, :], l_o[par][:, :]))
            i += 1
    p1.emit(nc)
    c1.close()

    if stages <= 1:
        ctx.close()
        return nc
    kvctx = ExitStack()
    ekv = kvctx.enter_context
    k_fm = ekv(nc.sbuf_tensor("k_fm", [128, c.G * c.TB], BF16))
    v_tm = ekv(nc.sbuf_tensor("v_tm", [128, (c.TB // 128) * c.KV], BF16))
    p2 = Prog()
    phase_start(p2)
    c2 = ExitStack()
    e2 = c2.enter_context
    wk_s = e2(nc.sbuf_tensor("wk_s", [128, c.NCC * c.KV], BF16))
    wv_s = e2(nc.sbuf_tensor("wv_s", [128, c.NCC * c.KV], BF16))

    x_fm = [e2(nc.sbuf_tensor(f"x_fm{i}", [128, c.NCC * 512], BF16, is_transpose_dest=True)) for i in range(2)]
    cs_b = [e2(nc.sbuf_tensor("cs_b0", [128, 1024], F32))] * 2
    k_f32 = e2(nc.sbuf_tensor("k_f32", [128, 512], F32))
    k_sq = e2(nc.sbuf_tensor("k_sq", [128, 512], F32))
    k_row = e2(nc.sbuf_tensor("k_row", [1, 512], F32))
    k_tb = e2(nc.sbuf_tensor("k_tb", [128, 512], F32))

    with nc.psum_tensor("ps_v0", [128, c.KV], F32) as ps_v0, \
         nc.psum_tensor("ps_v1", [128, c.KV], F32) as ps_v1, \
         nc.psum_tensor("ps_k0", [128, 512], F32) as ps_k0, \
         nc.psum_tensor("ps_k1", [128, 512], F32) as ps_k1, \
         nc.psum_tensor("ps_ss2", [1, 512], F32) as ps_ss2, \
         nc.psum_tensor("ps_bc2", [128, 512], F32) as ps_bc2, \
         nc.psum_tensor("ps_sw2", [128, 512], F32) as ps_sw2:
        ps_v = (ps_v0, ps_v1)
        ps_k = (ps_k0, ps_k1)

        for wsrc, wdst in ((wkT, wk_s), (wvT, wv_s)):
            for cc in range(c.NCC):
                kvw = dma(p2, "gpsimd", d[4], wdst[:, cc * c.KV:(cc + 1) * c.KV],
                          wsrc[cc * 128:(cc + 1) * 128, :])

        NT2 = c.TB // 512
        vevac, kdone, xfm_free = [], [], []
        p2.wait("tensor", d[4], kvw)
        for tt in range(NT2):
            par = tt % 2
            if tt >= 2:
                p2.wait("sync", st, xfm_free[tt - 2])
            for cc in range(c.NCC):
                dma(p2, "sync", d[par], x_fm[par][:, cc * 512:(cc + 1) * 512],
                    ln1_b[tt * 512:(tt + 1) * 512, cc * 128:(cc + 1) * 128], transpose=True)
            if tt >= 1:
                p2.wait("sync", sv, kdone[(tt - 1) * c.G + c.G - 1])
            dma(p2, "sync", d[par], cs_b[par][:, 0:512], cos_b[:, tt * 512:(tt + 1) * 512])
            tile_dma = dma(p2, "sync", d[par], cs_b[par][:, 512:1024],
                           sin_b[:, tt * 512:(tt + 1) * 512])

            p2.wait("tensor", d[par], tile_dma)
            p2.wait("vector", d[par], tile_dma)
            for v4 in range(4):
                vp = v4 % 2
                gi = tt * 4 + v4
                if gi >= 2:
                    p2.wait("tensor", sv, vevac[gi - 2])
                for cc in range(c.NCC):
                    p2.add("tensor", lambda E, cc=cc, v4=v4, vp=vp, b=x_fm[par]: E.matmul(
                        ps_v[vp][:, :],
                        b[:, cc * 512 + v4 * 128: cc * 512 + (v4 + 1) * 128],
                        wv_s[:, cc * c.KV:(cc + 1) * c.KV],
                        start=(cc == 0), stop=(cc == c.NCC - 1)))
                vm = p2.seal_tensor_inc(st)
                p2.wait("vector", st, vm)
                p2.add("vector", lambda E, gi=gi, vp=vp: E.tensor_copy(
                    v_tm[:, gi * c.KV:(gi + 1) * c.KV], ps_v[vp][:, :]))
                vevac.append(p2.seal_vector_inc(sv))

            for g in range(c.G):
                kp = g % 2
                gi = tt * c.G + g
                if gi >= 2:
                    p2.wait("tensor", sv, kdone[gi - 2])
                for cc in range(c.NCC):
                    p2.add("tensor", lambda E, cc=cc, g=g, kp=kp, b=x_fm[par]: E.matmul(
                        ps_k[kp][:, :],
                        wk_s[:, cc * c.KV + g * 128: cc * c.KV + (g + 1) * 128],
                        b[:, cc * 512:(cc + 1) * 512],
                        start=(cc == 0), stop=(cc == c.NCC - 1)))
                kmm = p2.seal_tensor_inc(st)
                if g == c.G - 1:
                    xfm_free.append(kmm)

                p2.wait("vector", st, kmm)
                p2.add("vector", lambda E, kp=kp: E.tensor_copy(
                    k_f32[:, :], ps_k[kp][:, :]))
                p2.add("vector", lambda E: E.tensor_mul(k_sq[:, :], k_f32[:, :], k_f32[:, :]))
                ksq_v = p2.seal_vector_inc(sv)
                p2.wait("tensor", sv, ksq_v)
                p2.add("tensor", lambda E: E.matmul(ps_ss2[0:1, :], onesf[:, :], k_sq[:, :],
                                                    start=True, stop=True).then_inc(st.h, 1))
                ssm = st.inc()
                p2.wait("scalar", st, ssm)
                p2.add("scalar", lambda E: E.activation(
                    k_row[0:1, :], ps_ss2[0:1, :], AF.Sqrt, scale=float(1.0 / 128.0))
                    .then_inc(sa.h, 1))
                srt = sa.inc()
                p2.wait("vector", sa, srt)
                p2.add("vector", lambda E: E.tensor_scalar(
                    k_row[0:1, :], k_row[0:1, :], 1e-6, None, ALU.add))
                vpad(p2)
                p2.add("vector", lambda E: E.reciprocal(k_row[0:1, :], k_row[0:1, :]))
                rr = p2.seal_vector_inc(sv)
                p2.wait("tensor", sv, rr)
                p2.add("tensor", lambda E: E.matmul(ps_bc2[:, :], ones1[0:1, :], k_row[0:1, :],
                                                    start=True, stop=True).then_inc(st.h, 1))
                bcm = st.inc()
                p2.wait("vector", st, bcm)
                p2.add("vector", lambda E: E.scalar_tensor_tensor(
                    k_f32[:, :], k_f32[:, :], knw_s[:, 0:1], ps_bc2[:, :], ALU.mult, ALU.mult))
                kn_v = p2.seal_vector_inc(sv)
                p2.wait("tensor", sv, kn_v)
                p2.add("tensor", lambda E: E.matmul(ps_sw2[:, :], swp_s[:, :], k_f32[:, :],
                                                    start=True, stop=True).then_inc(st.h, 1))
                sw_t = st.inc()
                p2.wait("vector", st, sw_t)
                p2.wait("vector", sv, kn_v)
                p2.add("vector", lambda E, b=cs_b[par]: E.tensor_mul(
                    k_f32[:, :], k_f32[:, :], b[:, 0:512]))
                p2.add("vector", lambda E, b=cs_b[par]: E.tensor_mul(
                    k_tb[:, :], ps_sw2[:, :], b[:, 512:1024]))
                p2.add("vector", lambda E, g=g, tt=tt: E.tensor_add(
                    k_fm[0:64, g * c.TB + tt * 512: g * c.TB + (tt + 1) * 512],
                    k_f32[0:64, :], k_tb[0:64, :]))
                p2.add("vector", lambda E, g=g, tt=tt: E.tensor_sub(
                    k_fm[64:128, g * c.TB + tt * 512: g * c.TB + (tt + 1) * 512],
                    k_f32[64:128, :], k_tb[64:128, :]))
                kdone.append(p2.seal_vector_inc(sv))
    p2.emit(nc)
    c2.close()

    if stages <= 2:
        ctx.close()
        return nc
    p3 = Prog()
    phase_start(p3)
    c3 = ExitStack()
    e3 = c3.enter_context
    import os as _os
    QH = int(_os.environ.get('QH', 6))
    q_fm = e3(nc.sbuf_tensor("q_fm", [128, c.H * c.QT], BF16))
    lnofm = e3(nc.sbuf_tensor("lnofm", [128, c.NCC * 512], BF16, is_transpose_dest=True))
    wq_sl = [e3(nc.sbuf_tensor(f"wq_sl{i}", [128, QH * 128], BF16)) for i in range(2)]
    qf32 = e3(nc.sbuf_tensor("qf32", [128, QH * 512], F32))
    q_sq = e3(nc.sbuf_tensor("q_sq", [128, 512], F32))
    q_row = e3(nc.sbuf_tensor("q_row", [1, 512], F32))
    q_ta = e3(nc.sbuf_tensor("q_ta", [128, 512], F32))
    q_tb = e3(nc.sbuf_tensor("q_tb", [128, 512], F32))
    cs_o = e3(nc.sbuf_tensor("cs_o", [128, 1024], F32))
    t_sb = [e3(nc.sbuf_tensor(f"t_sb{i}", [128, c.QT], F32)) for i in range(2)]
    p_sb = [e3(nc.sbuf_tensor(f"p_sb{i}", [128, c.QT], BF16)) for i in range(2)]
    y_f32 = e3(nc.sbuf_tensor("y_f32", [128, c.QT], F32))
    y_out = [e3(nc.sbuf_tensor(f"y_out{i}", [128, c.QT], BF16)) for i in range(2)]
    z_row = e3(nc.sbuf_tensor("z_row", [1, c.QT], F32))

    hchunks = []
    h0 = 0
    while h0 < c.H:
        hchunks.append((h0, min(QH, c.H - h0)))
        h0 += QH

    ydma = []
    yio = []
    wq_guard = {0: 0, 1: 0}
    prev_m_tensor_gate = 0
    prev_m_rope_gate = 0
    lnofm_gate = 0
    for m in range(c.NM):
        p3.wait("sync", st, lnofm_gate)
        p3.wait("sync", sv, prev_m_rope_gate)
        for cc in (range(c.NCC) if "nolno" not in abl else []):
            dma(p3, "sync", d[6], lnofm[:, cc * 512:(cc + 1) * 512],
                ln1_o[m * c.QT:(m + 1) * c.QT, cc * 128:(cc + 1) * 128], transpose=True)
        if "nocs" not in abl:
            dma(p3, "sync", d[6], cs_o[:, 0:512], cos_o[:, m * c.QT:(m + 1) * c.QT])
            lnodma = dma(p3, "sync", d[6], cs_o[:, 512:1024], sin_o[:, m * c.QT:(m + 1) * c.QT])
        else:
            lnodma = 0

        rope_last = 0
        p3.wait("tensor", sv, prev_m_tensor_gate)
        p3.wait("tensor", d[6], lnodma)
        p3.wait("vector", d[6], lnodma)
        for ci, (h0, nh) in enumerate(hchunks if "noqb" not in abl else []):
            with nc.psum_tensor(f"ps_q_{m}_{ci}", [128, QH * 512], F32) as ps_q:
                p3.wait("tensor", sv, rope_last)
                qmm_done = []
                for cc in range(c.NCC):
                    spar = cc % 2
                    p3.wait("sync", st, wq_guard[spar])
                    sl = dma(p3, "sync", d[spar], wq_sl[spar][:, 0:nh * 128],
                             wqT_h[cc * 128:(cc + 1) * 128, h0 * 128:(h0 + nh) * 128])
                    p3.wait("tensor", d[spar], sl)
                    for hh in range(nh):
                        p3.add("tensor", lambda E, cc=cc, hh=hh, spar=spar: E.matmul(
                            ps_q[:, hh * 512:(hh + 1) * 512],
                            wq_sl[spar][:, hh * 128:(hh + 1) * 128],
                            lnofm[:, cc * 512:(cc + 1) * 512],
                            start=(cc == 0), stop=(cc == c.NCC - 1)))
                    qmm_done.append(p3.seal_tensor_inc(st))
                    wq_guard[spar] = qmm_done[-1]
                p3.wait("vector", st, qmm_done[-1])
                for hh in range(nh):
                    p3.add("vector", lambda E, hh=hh: E.tensor_copy(
                        qf32[:, hh * 512:(hh + 1) * 512], ps_q[:, hh * 512:(hh + 1) * 512]))
                qevac = p3.seal_vector_inc(sv)
                if dbg and m == c.NM - 1 and ci == 0:
                    p3.wait("scalar", sv, qevac)
                    dma(p3, "scalar", d[7], q_dbg[:, 0:nh * 512], qf32[:, 0:nh * 512])
            with nc.psum_tensor(f"ps_ss3_{m}_{ci}", [1, 512], F32) as ps_ss3, \
                 nc.psum_tensor(f"ps_bc3_{m}_{ci}", [128, 512], F32) as ps_bc3, \
                 nc.psum_tensor(f"ps_sw3_{m}_{ci}", [128, 512], F32) as ps_sw3:
                p3.wait("tensor", sv, qevac)
                p3.wait("vector", sv, qevac)
                for hh in (range(nh) if "nonorm" not in abl else []):
                    h = h0 + hh
                    p3.add("vector", lambda E, hh=hh: E.tensor_mul(
                        q_sq[:, :], qf32[:, hh * 512:(hh + 1) * 512],
                        qf32[:, hh * 512:(hh + 1) * 512]))
                    qsq_v = p3.seal_vector_inc(sv)
                    p3.wait("tensor", sv, qsq_v)
                    p3.add("tensor", lambda E: E.matmul(
                        ps_ss3[0:1, :], onesf[:, :], q_sq[:, :],
                        start=True, stop=True).then_inc(st.h, 1))
                    ssm = st.inc()
                    p3.wait("scalar", st, ssm)
                    p3.add("scalar", lambda E: E.activation(
                        q_row[0:1, :], ps_ss3[0:1, :], AF.Sqrt, scale=float(1.0 / 128.0))
                        .then_inc(sa.h, 1))
                    p3.wait("vector", sa, sa.inc())
                    p3.add("vector", lambda E: E.tensor_scalar(
                        q_row[0:1, :], q_row[0:1, :], 1e-6, None, ALU.add))
                    vpad(p3)
                    p3.add("vector", lambda E: E.reciprocal(q_row[0:1, :], q_row[0:1, :]))
                    rr = p3.seal_vector_inc(sv)
                    p3.wait("tensor", sv, rr)
                    p3.add("tensor", lambda E: E.matmul(
                        ps_bc3[:, :], ones1[0:1, :], q_row[0:1, :],
                        start=True, stop=True).then_inc(st.h, 1))
                    bcm = st.inc()
                    p3.wait("vector", st, bcm)
                    p3.add("vector", lambda E, hh=hh: E.scalar_tensor_tensor(
                        qf32[:, hh * 512:(hh + 1) * 512], qf32[:, hh * 512:(hh + 1) * 512],
                        qnw_s[:, 0:1], ps_bc3[:, :], ALU.mult, ALU.mult))
                    qn_v = p3.seal_vector_inc(sv)
                    p3.wait("tensor", sv, qn_v)
                    p3.add("tensor", lambda E, hh=hh: E.matmul(
                        ps_sw3[:, :], swp_s[:, :], qf32[:, hh * 512:(hh + 1) * 512],
                        start=True, stop=True).then_inc(st.h, 1))
                    swq_t = st.inc()
                    p3.wait("vector", st, swq_t)
                    p3.wait("vector", sv, qn_v)
                    p3.add("vector", lambda E, hh=hh: E.tensor_mul(
                        q_ta[:, :], qf32[:, hh * 512:(hh + 1) * 512], cs_o[:, 0:512]))
                    p3.add("vector", lambda E: E.tensor_mul(
                        q_tb[:, :], ps_sw3[:, :], cs_o[:, 512:1024]))
                    p3.add("vector", lambda E, h=h: E.tensor_add(
                        q_fm[0:64, h * c.QT:(h + 1) * c.QT], q_ta[0:64, :], q_tb[0:64, :]))
                    p3.add("vector", lambda E, h=h: E.tensor_sub(
                        q_fm[64:128, h * c.QT:(h + 1) * c.QT], q_ta[64:128, :], q_tb[64:128, :]))
                    rope_last = p3.seal_vector_inc(sv)
        prev_m_rope_gate = rope_last
        lnofm_gate_next = None

        NB = (m + 1) * c.SPAN
        with nc.psum_tensor(f"ps_s0_{m}", [128, c.QT], F32) as ps_s0, \
             nc.psum_tensor(f"ps_s1_{m}", [128, c.QT], F32) as ps_s1, \
             nc.psum_tensor(f"ps_y0_{m}", [128, c.QT], F32) as ps_y0, \
             nc.psum_tensor(f"ps_y1_{m}", [128, c.QT], F32) as ps_y1, \
             nc.psum_tensor(f"ps_z0_{m}", [1, c.QT], F32) as ps_z0, \
             nc.psum_tensor(f"ps_z1_{m}", [1, c.QT], F32) as ps_z1, \
             nc.psum_tensor(f"ps_yb0_{m}", [128, c.QT], F32) as ps_yb0, \
             nc.psum_tensor(f"ps_yb1_{m}", [128, c.QT], F32) as ps_yb1:
            ps_s = (ps_s0, ps_s1)
            ps_y = (ps_y0, ps_y1)
            ps_z = (ps_z0, ps_z1)
            ps_yb = (ps_yb0, ps_yb1)
            tanh_done, pvz_done = {}, {}
            m_yio = []
            hlist = list(range(c.H))
            for h in (hlist if "noattn" not in abl else []):
                g = h // (c.H // c.G)
                hp = h % 2
                for b in range(NB):
                    par = b % 2
                    k = b - m * c.SPAN
                    lo = max(0, k) * c.CW
                    if (h, b - 2) in tanh_done:
                        p3.wait("tensor", sa, tanh_done[(h, b - 2)])
                    elif b < 2 and (h - 1, NB - 2 + b) in tanh_done:
                        p3.wait("tensor", sa, tanh_done[(h - 1, NB - 2 + b)])
                    if b == 0:
                        p3.wait("tensor", sv, rope_last)
                        if len(m_yio) >= 2:
                            p3.wait("tensor", sv, m_yio[-2])
                    p3.add("tensor", lambda E, g=g, h=h, b=b, par=par, lo=lo: E.matmul(
                        ps_s[par][:, lo:c.QT],
                        k_fm[:, g * c.TB + b * 128: g * c.TB + (b + 1) * 128],
                        q_fm[:, h * c.QT + lo:(h + 1) * c.QT],
                        start=True, stop=True).then_inc(st.h, 1))
                    qk_v = st.inc()
                    p3.wait("scalar", st, qk_v)
                    p3.add("scalar", lambda E, par=par, lo=lo: E.activation(
                        t_sb[par][:, lo:c.QT], ps_s[par][:, lo:c.QT],
                        AF.Tanh, scale=float(1.0 / (50.0 * math.sqrt(128.0))))
                        .then_inc(sa.h, 1))
                    tanh_done[(h, b)] = sa.inc()
                    if k >= 0:
                        p3.wait("vector", sa, tanh_done[(h, b)])
                        p3.add("vector", lambda E, par=par, lo=lo: E.tensor_tensor(
                            t_sb[par][:, lo:lo + c.CW], t_sb[par][:, lo:lo + c.CW],
                            mask_s[:, :], ALU.min))
                        mv = p3.seal_vector_inc(sv)
                        p3.wait("scalar", sv, mv)
                    if (h, b - 2) in pvz_done:
                        p3.wait("scalar", st, pvz_done[(h, b - 2)])
                    elif b < 2 and (h - 1, NB - 2 + b) in pvz_done:
                        p3.wait("scalar", st, pvz_done[(h - 1, NB - 2 + b)])
                    p3.add("scalar", lambda E, par=par, lo=lo: E.activation(
                        p_sb[par][:, lo:c.QT], t_sb[par][:, lo:c.QT], AF.Exp, scale=50.0)
                        .then_inc(sa.h, 1))
                    exp_v = sa.inc()
                    p3.wait("tensor", sa, exp_v)
                    if "nopv" not in abl:
                        p3.add("tensor", lambda E, g=g, b=b, hp=hp, par=par, lo=lo, NB=NB: E.matmul(
                            ps_y[hp][:, lo:c.QT],
                            v_tm[:, b * c.KV + g * 128: b * c.KV + (g + 1) * 128],
                            p_sb[par][:, lo:c.QT],
                            start=(b == 0), stop=(b == NB - 1)))
                    if "noz" not in abl:
                        p3.add("tensor", lambda E, b=b, hp=hp, par=par, lo=lo, NB=NB: E.matmul(
                            ps_z[hp][0:1, lo:c.QT],
                            ones_b[:, :], p_sb[par][:, lo:c.QT],
                            start=(b == 0), stop=(b == NB - 1)))
                    pvz_done[(h, b)] = p3.seal_tensor_inc(st)
                p3.wait("vector", st, pvz_done[(h, NB - 1)])
                if len(ydma) >= 2:
                    p3.wait("vector", d[2 + (h % 2)], ydma[-2])
                p3.add("vector", lambda E, hp=hp: E.tensor_copy(
                    y_f32[:, :], ps_y[hp][:, :]))
                p3.add("vector", lambda E, hp=hp: E.reciprocal(
                    z_row[0:1, :], ps_z[hp][0:1, :]))
                zr = p3.seal_vector_inc(sv)
                p3.wait("tensor", sv, zr)
                p3.add("tensor", lambda E, hp=hp: E.matmul(
                    ps_yb[hp][:, :], ones1[0:1, :], z_row[0:1, :],
                    start=True, stop=True).then_inc(st.h, 1))
                ybm = st.inc()
                p3.wait("vector", st, ybm)
                p3.add("vector", lambda E, hp=hp: E.tensor_mul(
                    y_out[hp][:, :], y_f32[:, :], ps_yb[hp][:, :]))
                yv = p3.seal_vector_inc(sv)
                yio.append(yv)
                m_yio.append(yv)
                p3.wait("scalar", sv, yv)
                ydma.append(dma(p3, "scalar", d[2 + hp],
                                y_sc[:, h * c.TO + m * c.QT: h * c.TO + (m + 1) * c.QT],
                                y_out[hp][:, :]))
            prev_m_tensor_gate = m_yio[-1] if m_yio else prev_m_tensor_gate
            lnofm_gate = pvz_done[(hlist[-1], NB - 1)] if pvz_done else lnofm_gate
    p3.emit(nc)
    c3.close()
    kvctx.close()

    if stages <= 3:
        ctx.close()
        return nc
    p4 = Prog()
    phase_start(p4)
    c4 = ExitStack()
    e4 = c4.enter_context
    lnb["scr"] = e4(nc.sbuf_tensor("l_scr4", [128, c.C], F32))
    lnb["st"] = e4(nc.sbuf_tensor("l_st4", [128, 12], F32))
    wo_sb = e4(nc.sbuf_tensor("wo_sb", [128, c.H * c.C], BF16))
    bo_bc = e4(nc.sbuf_tensor("bo_bc", [128, c.C], F32))
    bo_row = e4(nc.sbuf_tensor("bo_row", [1, c.C], F32))
    bo_tmp = e4(nc.sbuf_tensor("bo_tmp", [1, 512], F32))
    ysb = [e4(nc.sbuf_tensor(f"ysb{i}", [128, c.H * 128], BF16)) for i in range(2)]
    x0_sb = [e4(nc.sbuf_tensor(f"x0_sb{i}", [128, c.C], F32)) for i in range(2)]
    x1_sb = [e4(nc.sbuf_tensor(f"x1_sb{i}", [128, c.C], F32)) for i in range(2)]
    l2_sb = [e4(nc.sbuf_tensor(f"l2_sb{i}", [128, c.C], BF16)) for i in range(2)]

    with nc.psum_tensor("ps_o0", [128, 512], F32) as ps_o0, \
         nc.psum_tensor("ps_o1", [128, 512], F32) as ps_o1, \
         nc.psum_tensor("ps_bb", [128, 512], F32) as ps_bb:
        ps_o = (ps_o0, ps_o1)
        for h in (range(c.H) if "nowold" not in abl and "nowo" not in abl else []):
            dma(p4, "sync", d[6], wo_sb[:, h * c.C:(h + 1) * c.C], woT_h[h * 128:(h + 1) * 128, :])
        wo_dma = dma(p4, "sync", d[6], bo_row[:, :], bo_t[:, :])
        p4.wait("vector", d[6], wo_dma)
        bo_done_v = 0
        for co in (range(c.NCO) if "nobo" not in abl and "nowo" not in abl else []):
            p4.add("vector", lambda E, co=co: E.tensor_copy(
                bo_tmp[0:1, :], bo_row[0:1, co * 512:(co + 1) * 512]))
            bcp = p4.seal_vector_inc(sv)
            p4.wait("tensor", sv, bcp)
            p4.add("tensor", lambda E: E.matmul(
                ps_bb[:, :], ones1[0:1, :], bo_tmp[0:1, :],
                start=True, stop=True).then_inc(st.h, 1))
            bm = st.inc()
            p4.wait("vector", st, bm)
            p4.add("vector", lambda E, co=co: E.tensor_copy(
                bo_bc[:, co * 512:(co + 1) * 512], ps_bb[:, :]))
            bo_done_v = p4.seal_vector_inc(sv)

        ev_flat = []
        tile_done = []
        x1dma = []
        for tt in (range(c.NTT) if "noloop" not in abl else []):
            par = tt % 2
            if tt >= 2:
                p4.wait("sync", sv, tile_done[tt - 2])
                p4.wait("vector", d[2 + par], x1dma[tt - 2])
            for h in range(c.H):
                dma(p4, "sync", d[par], ysb[par][:, h * 128:(h + 1) * 128],
                    y_sc[:, h * c.TO + tt * 128: h * c.TO + (tt + 1) * 128])
            ytile_dma = dma(p4, "sync", d[par], x0_sb[par][:, :], x_o[tt * 128:(tt + 1) * 128, :])

            p4.wait("vector", d[par], ytile_dma)
            p4.wait("vector", sv, bo_done_v)
            p4.add("vector", lambda E, b=x0_sb[par]: E.tensor_add(b[:, :], b[:, :], bo_bc[:, :]))
            x0b_v = p4.seal_vector_inc(sv)

            p4.wait("tensor", d[par], ytile_dma)
            for co in (range(c.NCO) if "nooproj" not in abl else []):
                op = (tt * c.NCO + co) % 2
                gi = tt * c.NCO + co
                if gi >= 2:
                    p4.wait("tensor", sv, ev_flat[gi - 2])
                for h in range(c.H):
                    p4.add("tensor", lambda E, h=h, co=co, op=op, b=ysb[par]: E.matmul(
                        ps_o[op][:, :],
                        b[:, h * 128:(h + 1) * 128],
                        wo_sb[:, h * c.C + co * 512: h * c.C + (co + 1) * 512],
                        start=(h == 0), stop=(h == c.H - 1)))
                omm = p4.seal_tensor_inc(st)
                p4.wait("vector", st, omm)
                p4.wait("vector", sv, x0b_v)
                p4.add("vector", lambda E, co=co, op=op, xb=x0_sb[par], xo=x1_sb[par]:
                       E.tensor_add(xo[:, co * 512:(co + 1) * 512],
                                    xb[:, co * 512:(co + 1) * 512],
                                    ps_o[op][:, :]))
                ev_flat.append(p4.seal_vector_inc(sv))
            if "nooproj" in abl:
                p4.add("vector", lambda E, xb=x0_sb[par], xo=x1_sb[par]: E.tensor_copy(xo[:, :], xb[:, :]))
                ev_flat.append(p4.seal_vector_inc(sv))
            if "noln2" not in abl:
                p4.wait("vector", sv, ev_flat[-1] if ev_flat else 0)
                ln2v = ln_tile(p4, x1_sb[par], l2_sb[par], par, 1e-5)
            else:
                p4.add("vector", lambda E, b=l2_sb[par], xb=x1_sb[par]: E.tensor_copy(b[:, :], xb[:, 0:c.C]))
                ln2v = p4.seal_vector_inc(sv)
            tile_done.append(ln2v)
            p4.wait("scalar", sv, ln2v)
            dma(p4, "scalar", d[2 + par], x1_o[tt * 128:(tt + 1) * 128, :], x1_sb[par][:, :])
            x1dma.append(dma(p4, "scalar", d[2 + par], ln2_o[tt * 128:(tt + 1) * 128, :], l2_sb[par][:, :]))
    p4.emit(nc)
    c4.close()

    if stages <= 4:
        ctx.close()
        return nc
    p5 = Prog()
    phase_start(p5)
    c5 = ExitStack()
    e5 = c5.enter_context
    l2fm = e5(nc.sbuf_tensor("l2fm", [128, c.NCC * c.TO], BF16, is_transpose_dest=True))
    g_sl = [e5(nc.sbuf_tensor(f"g_sl{i}", [128, c.NCC * 512], BF16, is_transpose_dest=True)) for i in range(2)]
    u_sl = [e5(nc.sbuf_tensor(f"u_sl{i}", [128, c.NCC * 512], BF16, is_transpose_dest=True)) for i in range(2)]
    sg_t = [e5(nc.sbuf_tensor(f"sg_t{i}", [128, 512], F32)) for i in range(2)]
    h_t = [e5(nc.sbuf_tensor(f"h_t{i}", [128, 512], BF16)) for i in range(2)]
    h_sq = [e5(nc.sbuf_tensor(f"h_sq{i}", [128, 512], F32)) for i in range(2)]
    s2row = e5(nc.sbuf_tensor("s2row", [1, c.TO], F32))
    NQ4 = c.TO // 512

    with nc.psum_tensor("ps_g0", [128, 512], F32) as ps_g0, \
         nc.psum_tensor("ps_g1", [128, 512], F32) as ps_g1, \
         nc.psum_tensor("ps_u0", [128, 512], F32) as ps_u0, \
         nc.psum_tensor("ps_u1", [128, 512], F32) as ps_u1, \
         nc.psum_tensor("ps_s2", [1, NQ4 * 512], F32) as ps_s2:
        ps_g = (ps_g0, ps_g1)
        ps_u = (ps_u0, ps_u1)
        for cc in range(c.NCC):
            l2_dma = dma(p5, "sync", d[6], l2fm[:, cc * c.TO:(cc + 1) * c.TO],
                         ln2_o[:, cc * 128:(cc + 1) * 128], transpose=True)

        h_done, hdma, s2mm_l = [], [], []
        gu_mm_last = {}
        s2mm = 0
        for iu4 in range(c.NU // 4):
            par = iu4 % 2
            if iu4 >= 2:
                p5.wait("sync", st, gu_mm_last[iu4 - 2])
            slab_last = 0
            for base, buf in ((iu4 * 4 * 128, g_sl[par]), (c.U + iu4 * 4 * 128, u_sl[par])):
                for cc in range(c.NCC):
                    slab_last = dma(p5, "sync", d[par], buf[:, cc * 512:(cc + 1) * 512],
                                    wgu_q[base:base + 512, cc * 128:(cc + 1) * 128],
                                    transpose=True)
            p5.wait("tensor", d[par], slab_last)
            if iu4 == 0:
                p5.wait("tensor", d[6], l2_dma)
            for sub in range(4):
                iu = iu4 * 4 + sub
                for tq in range(NQ4):
                    gi = iu * NQ4 + tq
                    gp = gi % 2
                    if gi >= 2:
                        p5.wait("tensor", sv, h_done[gi - 2])
                    for cc in range(c.NCC):
                        p5.add("tensor", lambda E, cc=cc, sub=sub, tq=tq, gp=gp, b=g_sl[par]:
                               E.matmul(ps_g[gp][:, :],
                                        b[:, cc * 512 + sub * 128: cc * 512 + (sub + 1) * 128],
                                        l2fm[:, cc * c.TO + tq * 512: cc * c.TO + (tq + 1) * 512],
                                        start=(cc == 0), stop=(cc == c.NCC - 1)))
                    for cc in range(c.NCC):
                        p5.add("tensor", lambda E, cc=cc, sub=sub, tq=tq, gp=gp, b=u_sl[par]:
                               E.matmul(ps_u[gp][:, :],
                                        b[:, cc * 512 + sub * 128: cc * 512 + (sub + 1) * 128],
                                        l2fm[:, cc * c.TO + tq * 512: cc * c.TO + (tq + 1) * 512],
                                        start=(cc == 0), stop=(cc == c.NCC - 1)))
                    gu = p5.seal_tensor_inc(st)
                    gu_mm_last[iu4] = gu
                    p5.wait("scalar", st, gu)
                    if gi >= 2:
                        p5.wait("scalar", sv, h_done[gi - 2])
                    p5.add("scalar", lambda E, gp=gp, sg=sg_t[gi % 2]: E.activation(
                        sg[:, :], ps_g[gp][:, :], AF.Silu)
                        .then_inc(sa.h, 1))
                    silu_v = sa.inc()
                    p5.wait("vector", sa, silu_v)
                    if gi >= 2:
                        p5.wait("vector", d[2 + (gi % 2)], hdma[gi - 2])
                        p5.wait("vector", st, s2mm_l[gi - 2])
                    hb = h_t[gi % 2]
                    p5.add("vector", lambda E, gp=gp, hb=hb, sg=sg_t[gi % 2]: E.tensor_mul(
                        hb[:, :], sg[:, :], ps_u[gp][:, :]))
                    p5.add("vector", lambda E, hb=hb, hq=h_sq[gi % 2]: E.tensor_mul(
                        hq[:, :], hb[:, :], hb[:, :]))
                    h_done.append(p5.seal_vector_inc(sv))
                    p5.wait("tensor", sv, h_done[-1])
                    p5.add("tensor", lambda E, tq=tq, iu=iu, hq=h_sq[gi % 2]: E.matmul(
                        ps_s2[0:1, tq * 512:(tq + 1) * 512], onesf[:, :], hq[:, :],
                        start=(iu == 0), stop=(iu == c.NU - 1)).then_inc(st.h, 1))
                    s2mm = st.inc()
                    s2mm_l.append(s2mm)
                    p5.wait("scalar", sv, h_done[-1])
                    hdma.append(dma(p5, "scalar", d[2 + (gi % 2)],
                                    h_sc[iu * 128:(iu + 1) * 128, tq * 512:(tq + 1) * 512],
                                    hb[:, :]))
        p5.wait("scalar", st, s2mm)
        p5.add("scalar", lambda E: E.activation(
            s2row[0:1, :], ps_s2[0:1, 0:c.TO], AF.Sqrt, bias=eps_ffn[0:1, 0:1],
            scale=float(1.0 / c.U)).then_inc(sa.h, 1))
        p5.wait("vector", sa, sa.inc())
        vpad(p5)
        p5.add("vector", lambda E: E.reciprocal(s2row[0:1, :], s2row[0:1, :]))
        iv = p5.seal_vector_inc(sv)
        p5.wait("scalar", sv, iv)
        dma(p5, "scalar", d[7], inv_sc[:, :], s2row[0:1, :])
    p5.emit(nc)
    c5.close()

    if stages <= 5:
        ctx.close()
        return nc
    p6 = Prog()
    phase_start(p6)
    c6 = ExitStack()
    e6 = c6.enter_context
    wp_sl = e6(nc.sbuf_tensor("wp_sl", [128, c.NU * 512], BF16, is_transpose_dest=True))
    h_sl = [e6(nc.sbuf_tensor(f"h_sl{i}", [128, c.NU * 128], BF16)) for i in range(2)]
    invcol = e6(nc.sbuf_tensor("invcol", [128, c.NTT], F32))
    x1r = [e6(nc.sbuf_tensor(f"x1r{i}", [128, 512], F32)) for i in range(2)]
    o_sb = [e6(nc.sbuf_tensor(f"o_sb{i}", [128, 512], F32)) for i in range(2)]

    with nc.psum_tensor("ps_f0", [128, 512], F32) as ps_f0, \
         nc.psum_tensor("ps_f1", [128, 512], F32) as ps_f1:
        ps_f = (ps_f0, ps_f1)
        for tt in range(c.NTT):
            inv_dma = dma(p6, "sync", d[6], invcol[:, tt:tt + 1],
                          inv_sc[0:1, tt * 128:(tt + 1) * 128].rearrange("o p -> p o"))
        fmm_done, fev_done, odma = [], [], []
        co_mm_last = 0
        for co in range(c.NCO):
            if co >= 1:
                p6.wait("sync", st, co_mm_last)
            for iu in range(c.NU):
                wp_dma = dma(p6, "sync", d[7], wp_sl[:, iu * 512:(iu + 1) * 512],
                             wp_q[co * 512:(co + 1) * 512, iu * 128:(iu + 1) * 128],
                             transpose=True)
            for tt in range(c.NTT):
                par = tt % 2
                gi = co * c.NTT + tt
                if gi >= 2:
                    p6.wait("sync", st, fmm_done[gi - 2])
                for iu in range(c.NU):
                    dma(p6, "sync", d[par], h_sl[par][:, iu * 128:(iu + 1) * 128],
                        h_sc[iu * 128:(iu + 1) * 128, tt * 128:(tt + 1) * 128])
                hx_dma = dma(p6, "sync", d[par], x1r[par][:, :],
                             x1_o[tt * 128:(tt + 1) * 128, co * 512:(co + 1) * 512])

                p6.wait("tensor", d[par], hx_dma)
                p6.wait("tensor", d[7], wp_dma)
                if gi >= 2:
                    p6.wait("tensor", sv, fev_done[gi - 2])
                fp = gi % 2
                for iu in range(c.NU):
                    p6.add("tensor", lambda E, iu=iu, fp=fp, b=h_sl[par]: E.matmul(
                        ps_f[fp][:, :],
                        b[:, iu * 128:(iu + 1) * 128],
                        wp_sl[:, iu * 512:(iu + 1) * 512],
                        start=(iu == 0), stop=(iu == c.NU - 1)))
                fmm_done.append(p6.seal_tensor_inc(st))
                p6.wait("vector", st, fmm_done[-1])
                p6.wait("vector", d[par], hx_dma)
                p6.wait("vector", d[6], inv_dma)
                if gi >= 2:
                    p6.wait("vector", d[2 + par], odma[gi - 2])
                p6.add("vector", lambda E, tt=tt, fp=fp, par=par: E.scalar_tensor_tensor(
                    o_sb[par][:, :], ps_f[fp][:, :], invcol[:, tt:tt + 1],
                    x1r[par][:, :], ALU.mult, ALU.add))
                fev_done.append(p6.seal_vector_inc(sv))
                p6.wait("scalar", sv, fev_done[-1])
                odma.append(dma(p6, "scalar", d[2 + par],
                                out_t[tt * 128:(tt + 1) * 128, co * 512:(co + 1) * 512],
                                o_sb[par][:, :]))
            co_mm_last = fmm_done[-1]
        for eng in ENGS:
            p6.wait(eng, d[2], d[2].n)
            p6.wait(eng, d[3], d[3].n)
    p6.emit(nc)
    c6.close()

    ctx.close()
    return nc


def _rope_tables(T, D=128):
    ar = np.arange(0, D, 2, dtype=np.float32)
    inv_freq = (np.float32(1.0) /
                np.power(np.float32(10000.0), ar / np.float32(D))).astype(np.float32)
    t = np.arange(T, dtype=np.float32)
    freqs = np.outer(t, inv_freq).astype(np.float32)
    return np.cos(freqs).astype(np.float32), np.sin(freqs).astype(np.float32)


def make_inputs(cfg, core, x, Wq, Wk, Wv, Wo, bo, qnw, knw, Wgu, Wp):
    c = cfg
    b, p = core // 4, core % 4
    cosT, sinT = _rope_tables(c.TB, 128)
    cosD = np.concatenate([cosT, cosT], axis=1)
    sinD = np.concatenate([sinT, sinT], axis=1)
    jj = np.arange(128)[:, None]
    ii = np.arange(c.CW)[None, :]
    mask = np.where(p + c.S * ii >= jj, np.float32(1e30), np.float32(-1.0)).astype(np.float32)
    swp = np.zeros((128, 128), np.float32)
    for d in range(64):
        swp[64 + d, d] = 1.0
        swp[d, 64 + d] = 1.0
    return {
        "x_b": np.ascontiguousarray(x[b]),
        "x_o": np.ascontiguousarray(x[b, p::c.S]),
        "wqT": np.ascontiguousarray(Wq.T),
        "wkT": np.ascontiguousarray(Wk.T),
        "wvT": np.ascontiguousarray(Wv.T),
        "woT": np.ascontiguousarray(Wo.T),
        "bo": np.ascontiguousarray(bo.reshape(1, -1)),
        "qnw": np.ascontiguousarray(qnw.reshape(128, 1)),
        "knw": np.ascontiguousarray(knw.reshape(128, 1)),
        "cos_b": np.ascontiguousarray(cosD.T),
        "sin_b": np.ascontiguousarray(sinD.T),
        "cos_o": np.ascontiguousarray(cosD[p::c.S].T),
        "sin_o": np.ascontiguousarray(sinD[p::c.S].T),
        "mask": mask,
        "swp": swp,
        "wgu": np.ascontiguousarray(Wgu),
        "wp": np.ascontiguousarray(Wp),
    }


def kernel(L0_attn_x, L0_attn_Wq, L0_attn_Wk, L0_attn_Wv, L0_attn_Wo, L0_attn_bo,
           L0_attn_q_norm_w, L0_attn_k_norm_w, L0_ffn_w_gate_up, L0_ffn_w_proj):
    cfg = Cfg()
    x = np.asarray(L0_attn_x, dtype=np.float32)
    nc = build_nc(cfg)
    in_maps = [make_inputs(cfg, core, x, np.asarray(L0_attn_Wq), np.asarray(L0_attn_Wk),
                           np.asarray(L0_attn_Wv), np.asarray(L0_attn_Wo),
                           np.asarray(L0_attn_bo), np.asarray(L0_attn_q_norm_w),
                           np.asarray(L0_attn_k_norm_w), np.asarray(L0_ffn_w_gate_up),
                           np.asarray(L0_ffn_w_proj))
               for core in range(8)]
    res = run_bass_kernel_spmd(nc, in_maps, core_ids=list(range(8)))
    out = np.empty((2, cfg.TB, cfg.C), dtype=np.float32)
    for core in range(8):
        b, p = core // 4, core % 4
        out[b, p::cfg.S, :] = res.results[core]["out"]
    return out


# version 06
# speedup vs baseline: 1.4961x; 1.4961x over previous
import math
from contextlib import ExitStack

import numpy as np
import ml_dtypes
import concourse.bass as bass
import concourse.mybir as mybir
from concourse.bass_utils import run_bass_kernel_spmd

bf16 = ml_dtypes.bfloat16
F32 = mybir.dt.float32
BF16 = mybir.dt.bfloat16
AF = mybir.ActivationFunctionType
ALU = mybir.AluOpType
AX = mybir.AxisListType

F32EPS = float(np.finfo(np.float32).eps)
ENGS = ("sync", "vector", "scalar", "tensor", "gpsimd")


class Cfg:
    def __init__(self, TB=8192, C=2048, H=16, G=4, U=8192):
        self.S = 4
        self.TB = TB
        self.TO = TB // self.S
        self.C = C
        self.H = H
        self.G = G
        self.KV = G * 128
        self.U = U
        self.QT = 512
        self.KB = 128
        self.CW = self.KB // self.S
        self.QG = 64
        self.NCC = C // 128
        self.NM = self.TO // self.QT
        self.SPAN = self.S * self.QT // self.KB
        self.NU = U // 128
        self.NCO = C // 512
        self.NTT = self.TO // 128
        assert self.H * 128 == C and self.TO % self.QT == 0
        assert C % 512 == 0 and U % 512 == 0 and TB % 512 == 0
        assert self.SPAN % 2 == 0


class Sem:
    def __init__(self, handle):
        self.h = handle
        self.n = 0

    def inc(self, d=1):
        self.n += d
        return self.n


class Prog:
    def __init__(self):
        self.q = {k: [] for k in ENGS}

    def add(self, eng, fn):
        self.q[eng].append(fn)

    def wait(self, eng, sem, val):
        if val > 0:
            f = lambda E, s=sem.h, v=val: E.wait_ge(s, v)
            f.is_wait = True
            self.q[eng].append(f)

    def seal_tensor_inc(self, sem):
        f = self.q["tensor"][-1]
        v = sem.inc()
        self.q["tensor"][-1] = lambda E, f=f, s=sem.h: f(E).then_inc(s, 1)
        return v

    def seal_vector_inc(self, sem):
        f = self.q["vector"][-1]
        v = sem.inc()
        self.q["vector"][-1] = lambda E, f=f, s=sem.h: f(E).then_inc(s, 1)
        return v

    def emit(self, nc):
        with nc.Block() as block:
            for eng in ENGS:
                ops = self.q[eng]
                if not ops or all(getattr(f, "is_wait", False) for f in ops):
                    continue

                def body(E, ops=ops):
                    for f in ops:
                        f(E)

                getattr(block, eng)(body)


def build_nc(cfg: Cfg, dbg=False, stages=9, abl=()):
    c = cfg
    nc = bass.Bass("TRN2", target_bir_lowering=False, detect_race_conditions=False)

    x_b = nc.dram_tensor("x_b", [c.TB, c.C], F32, kind="ExternalInput")
    x_o = nc.dram_tensor("x_o", [c.TO, c.C], F32, kind="ExternalInput")
    wqT = nc.dram_tensor("wqT", [c.C, c.C], F32, kind="ExternalInput")
    wkT = nc.dram_tensor("wkT", [c.C, c.KV], F32, kind="ExternalInput")
    wvT = nc.dram_tensor("wvT", [c.C, c.KV], F32, kind="ExternalInput")
    woT = nc.dram_tensor("woT", [c.C, c.C], F32, kind="ExternalInput")
    bo_t = nc.dram_tensor("bo", [1, c.C], F32, kind="ExternalInput")
    qnw_t = nc.dram_tensor("qnw", [128, 1], F32, kind="ExternalInput")
    knw_t = nc.dram_tensor("knw", [128, 1], F32, kind="ExternalInput")
    cos_b = nc.dram_tensor("cos_b", [128, c.TB], F32, kind="ExternalInput")
    sin_b = nc.dram_tensor("sin_b", [128, c.TB], F32, kind="ExternalInput")
    cos_o = nc.dram_tensor("cos_o", [128, c.TO], F32, kind="ExternalInput")
    sin_o = nc.dram_tensor("sin_o", [128, c.TO], F32, kind="ExternalInput")
    mask_t = nc.dram_tensor("mask", [128, c.CW], F32, kind="ExternalInput")
    swp_t = nc.dram_tensor("swp", [128, 128], F32, kind="ExternalInput")
    wgu = nc.dram_tensor("wgu", [2 * c.U, c.C], F32, kind="ExternalInput")
    wp = nc.dram_tensor("wp", [c.C, c.U], F32, kind="ExternalInput")
    out_t = nc.dram_tensor("out", [c.TO, c.C], F32, kind="ExternalOutput")

    kk = dict(kind="ExternalOutput") if dbg else {}
    ln1_b = nc.dram_tensor("ln1_b", [c.TB, c.C], BF16, **kk)
    ln1_o = nc.dram_tensor("ln1_o", [c.TO, c.C], BF16, **kk)
    ln2_o = nc.dram_tensor("ln2_o", [c.TO, c.C], BF16, **kk)
    x1_o = nc.dram_tensor("x1_o", [c.TO, c.C], F32, **kk)
    wgu_q = nc.dram_tensor("wgu_q", [2 * c.U, c.C], BF16, **kk)
    wp_q = nc.dram_tensor("wp_q", [c.C, c.U], BF16, **kk)
    wqT_h = nc.dram_tensor("wqT_h", [c.C, c.C], BF16, **kk)
    woT_h = nc.dram_tensor("woT_h", [c.C, c.C], BF16, **kk)
    h_sc = nc.dram_tensor("h_sc", [c.U, c.TO], BF16, **kk)
    y_sc = nc.dram_tensor("y_sc", [128, c.H * c.TO], BF16, **kk)
    inv_sc = nc.dram_tensor("inv_sc", [1, c.TO], F32, **kk)
    q_dbg = nc.dram_tensor("q_dbg", [128, 4096], F32, **kk) if dbg else None

    ctx = ExitStack()
    e = ctx.enter_context

    ones_b = e(nc.sbuf_tensor("ones_b", [128, 1], BF16))
    ones1 = e(nc.sbuf_tensor("ones1", [1, 128], F32))
    onesf = e(nc.sbuf_tensor("onesf", [128, 1], F32))
    qnw_s = e(nc.sbuf_tensor("qnw_s", [128, 1], F32))
    knw_s = e(nc.sbuf_tensor("knw_s", [128, 1], F32))
    mask_s = e(nc.sbuf_tensor("mask_s", [128, c.CW], F32))
    swp_s = e(nc.sbuf_tensor("swp_s", [128, 128], F32))
    lnb = {}
    eps_ln = e(nc.sbuf_tensor("eps_ln", [128, 1], F32))
    padt = e(nc.sbuf_tensor("padt", [128, 8], F32))
    padt2 = e(nc.sbuf_tensor("padt2", [128, 8], F32))
    invC = e(nc.sbuf_tensor("invC", [128, 1], F32))
    eps_ffn = e(nc.sbuf_tensor("eps_ffn", [128, 1], F32))


    d = [Sem(e(nc.semaphore(f"d{i}"))) for i in range(8)]
    sv = Sem(e(nc.semaphore("sv")))
    sp = Sem(e(nc.semaphore("sp")))
    sa = Sem(e(nc.semaphore("sa")))
    st = Sem(e(nc.semaphore("st")))
    ALLSEMS = tuple(d) + (sv, sp, sa, st)

    def phase_start(p: Prog):
        for eng in ENGS:
            for s in ALLSEMS:
                p.wait(eng, s, s.n)

    def vpad(p, n=5):
        for _ in range(n):
            p.add("vector", lambda E: E.tensor_copy(padt[:, :], padt2[:, :]))

    def dma(p, eng, sem, out, in_, transpose=False):
        if transpose:
            p.add(eng, lambda E, o=out, i=in_, s=sem.h:
                  E.dma_start_transpose(o, i).then_inc(s, 16))
        else:
            p.add(eng, lambda E, o=out, i=in_, s=sem.h:
                  E.dma_start(out=o, in_=i).then_inc(s, 16))
        return sem.inc(16)

    pQ = Prog()
    qctx = ExitStack()
    eq = qctx.enter_context
    QW = 2048
    qtiles = []
    for r0 in range(0, 2 * c.U, 128):
        for c0 in range(0, c.C, QW):
            qtiles.append((wgu, wgu_q, r0, c0, min(QW, c.C - c0)))
    for r0 in range(0, c.C, 128):
        for c0 in range(0, c.U, QW):
            qtiles.append((wp, wp_q, r0, c0, min(QW, c.U - c0)))

    q_win = [eq(nc.sbuf_tensor(f"q_win{i}", [128, QW], F32)) for i in range(2)]
    q_wb = eq(nc.sbuf_tensor("q_wb", [128, QW], BF16))
    q_sum = eq(nc.sbuf_tensor("q_sum", [128, QW // c.QG], F32))
    q_scl = eq(nc.sbuf_tensor("q_scl", [128, QW // c.QG], BF16))
    q_rcp = eq(nc.sbuf_tensor("q_rcp", [128, QW // c.QG], F32))
    q_sclf = eq(nc.sbuf_tensor("q_sclf", [128, QW // c.QG], F32))
    q_qq = eq(nc.sbuf_tensor("q_qq", [128, QW], BF16))
    q_wh = eq(nc.sbuf_tensor("q_wh", [128, QW], BF16))
    q_out = [eq(nc.sbuf_tensor(f"q_out{i}", [128, QW], BF16)) for i in range(2)]
    c_wq = [eq(nc.sbuf_tensor(f"c_wq{i}", [128, c.C], BF16)) for i in range(2)]

    dma(pQ, "sync", d[6], qnw_s[:, :], qnw_t[:, :])
    dma(pQ, "sync", d[6], knw_s[:, :], knw_t[:, :])
    dma(pQ, "sync", d[6], mask_s[:, :], mask_t[:, :])
    dma(pQ, "sync", d[6], swp_s[:, :], swp_t[:, :])
    pQ.add("vector", lambda E: E.memset(ones_b[:, :], 1.0))
    pQ.add("vector", lambda E: E.memset(ones1[:, :], 1.0))
    pQ.add("vector", lambda E: E.memset(onesf[:, :], 1.0))
    pQ.add("vector", lambda E: E.memset(eps_ln[:, :], 1e-5))
    pQ.add("vector", lambda E: E.memset(eps_ffn[:, :], F32EPS))
    pQ.add("vector", lambda E: E.memset(invC[:, :], 1.0 / c.C))
    pQ.add("vector", lambda E: E.memset(padt2[:, :], 0.0))

    q_winfree, q_outrdy, q_outdma = [], [], []
    for i, (srct, dstt, r0, c0, w) in enumerate(qtiles):
        g = w // c.QG
        if i >= 2:
            pQ.wait("sync", sv, q_winfree[i - 2])
        qin = dma(pQ, "sync", d[i % 2], q_win[i % 2][:, 0:w], srct[r0:r0 + 128, c0:c0 + w])

        pQ.wait("vector", d[i % 2], qin)
        if i >= 2:
            pQ.wait("vector", d[2 + (i % 2)], q_outdma[i - 2])
        wi, ob = q_win[i % 2], q_out[i % 2]
        pQ.add("vector", lambda E, wi=wi, w=w: E.tensor_copy(q_wb[:, 0:w], wi[:, 0:w]))
        q_winfree.append(pQ.seal_vector_inc(sv))
        pQ.wait("vector", sv, q_winfree[i])
        pQ.add("vector", lambda E, w=w, g=g: E.tensor_reduce(
            q_sum[:, 0:g], q_wb[:, 0:w].rearrange("p (g x) -> p g x", x=c.QG),
            AX.X, ALU.add, apply_absolute_value=True))
        vpad(pQ)
        pQ.add("vector", lambda E, g=g: E.tensor_scalar(
            q_sclf[:, 0:g], q_sum[:, 0:g], 1.0 / c.QG, 1e-8, ALU.mult, ALU.max))
        vpad(pQ)
        pQ.add("vector", lambda E, g=g: E.tensor_copy(q_scl[:, 0:g], q_sclf[:, 0:g]))
        vpad(pQ)
        pQ.add("vector", lambda E, g=g: E.tensor_copy(q_sclf[:, 0:g], q_scl[:, 0:g]))
        vpad(pQ)
        pQ.add("vector", lambda E, g=g: E.reciprocal(q_rcp[:, 0:g], q_sclf[:, 0:g]))
        vpad(pQ)
        pQ.add("vector", lambda E, w=w, g=g: E.tensor_tensor(
            q_qq[:, 0:w].rearrange("p (g x) -> p g x", x=c.QG),
            q_wb[:, 0:w].rearrange("p (g x) -> p g x", x=c.QG),
            q_rcp[:, 0:g].broadcast_to([128, g, c.QG]), ALU.mult))
        pQ.add("vector", lambda E, w=w: E.tensor_scalar(
            q_qq[:, 0:w], q_qq[:, 0:w], 192.0, None, ALU.add))
        pQ.add("vector", lambda E, w=w: E.tensor_scalar(
            q_qq[:, 0:w], q_qq[:, 0:w], 192.0, 1.0, ALU.subtract, ALU.min))
        pQ.add("vector", lambda E, w=w: E.tensor_scalar(
            q_qq[:, 0:w], q_qq[:, 0:w], -1.0, None, ALU.max))
        pQ.add("vector", lambda E, w=w, g=g: E.tensor_tensor(
            q_wh[:, 0:w].rearrange("p (g x) -> p g x", x=c.QG),
            q_qq[:, 0:w].rearrange("p (g x) -> p g x", x=c.QG),
            q_sclf[:, 0:g].broadcast_to([128, g, c.QG]), ALU.mult))
        pQ.add("vector", lambda E, w=w: E.tensor_sub(q_wh[:, 0:w], q_wh[:, 0:w], q_wb[:, 0:w]))
        pQ.add("vector", lambda E, ob=ob, w=w: E.tensor_add(
            ob[:, 0:w], q_wb[:, 0:w], q_wh[:, 0:w]))
        q_outrdy.append(pQ.seal_vector_inc(sv))

        pQ.wait("scalar", sv, q_outrdy[i])
        q_outdma.append(dma(pQ, "scalar", d[2 + (i % 2)], dstt[r0:r0 + 128, c0:c0 + w], ob[:, 0:w]))

    ctiles = [(wqT, wqT_h, r0) for r0 in range(0, c.C, 128)] + \
             [(woT, woT_h, r0) for r0 in range(0, c.C, 128)]
    cw_store = []
    for i, (srct, dstt, r0) in enumerate(ctiles):
        ds = d[4 + (i % 2)]
        if i >= 2:
            pQ.wait("gpsimd", ds, cw_store[i - 2])
        v = dma(pQ, "gpsimd", ds, c_wq[i % 2][:, :], srct[r0:r0 + 128, :])
        pQ.wait("gpsimd", ds, v)
        cw_store.append(dma(pQ, "gpsimd", ds, dstt[r0:r0 + 128, :], c_wq[i % 2][:, :]))
    pQ.emit(nc)
    qctx.close()

    p1 = Prog()
    phase_start(p1)
    c1 = ExitStack()
    e1 = c1.enter_context
    lnb["scr"] = e1(nc.sbuf_tensor("l_scr1", [128, c.C], F32))
    lnb["st"] = e1(nc.sbuf_tensor("l_st1", [128, 12], F32))
    l_x = [e1(nc.sbuf_tensor(f"l_x{i}", [128, c.C], F32)) for i in range(2)]
    l_o = [e1(nc.sbuf_tensor(f"l_o{i}", [128, c.C], BF16)) for i in range(2)]

    def ln_tile(p: Prog, xbuf, obuf, par, eps):
        o = 6 * par
        l_scr, l_st = lnb["scr"], lnb["st"]
        p.add("vector", lambda E: E.reduce_sum(
            l_st[:, o + 0:o + 1], xbuf[:, :], AX.X))
        p.add("vector", lambda E: E.tensor_mul(l_scr[:, :], xbuf[:, :], xbuf[:, :]))
        p.add("vector", lambda E: E.reduce_sum(
            l_st[:, o + 1:o + 2], l_scr[:, :], AX.X))
        vred = p.seal_vector_inc(sv)
        p.wait("gpsimd", sv, vred)
        p.add("gpsimd", lambda E: E.tensor_mul(
            l_st[:, o + 2:o + 3], l_st[:, o + 0:o + 1], invC[:, :]))
        p.add("gpsimd", lambda E: E.tensor_mul(
            l_st[:, o + 3:o + 4], l_st[:, o + 2:o + 3], l_st[:, o + 2:o + 3]))
        p.add("gpsimd", lambda E: E.tensor_mul(
            l_st[:, o + 4:o + 5], l_st[:, o + 1:o + 2], invC[:, :]))
        p.add("gpsimd", lambda E: E.tensor_sub(
            l_st[:, o + 4:o + 5], l_st[:, o + 4:o + 5], l_st[:, o + 3:o + 4])
            .then_inc(sp.h, 1))
        gva = sp.inc()
        p.wait("scalar", sp, gva)
        p.add("scalar", lambda E: E.activation(
            l_st[:, o + 5:o + 6], l_st[:, o + 4:o + 5], AF.Sqrt, bias=eps_ln[:, 0:1], scale=1.0)
            .then_inc(sa.h, 1))
        saa = sa.inc()
        p.wait("vector", sa, saa)
        vpad(p)
        p.add("vector", lambda E: E.reciprocal(l_st[:, o + 5:o + 6], l_st[:, o + 5:o + 6]))
        vpad(p, 6)
        p.add("vector", lambda E: E.tensor_scalar(
            obuf[:, :], xbuf[:, :], l_st[:, o + 2:o + 3], l_st[:, o + 5:o + 6],
            ALU.subtract, ALU.mult))
        return p.seal_vector_inc(sv)

    ln_xfree, ln_odma = [], []
    ln_srcs = [(x_b, ln1_b, c.TB // 128)] + [(x_o, ln1_o, c.TO // 128)]
    i = 0
    for src_dram, dst_dram, ntiles in ln_srcs:
        for j in range(ntiles):
            par = i % 2
            if i >= 2:
                p1.wait("sync", sv, ln_xfree[i - 2])
            lin = dma(p1, "sync", d[par], l_x[par][:, :], src_dram[j * 128:(j + 1) * 128, :])
            p1.wait("vector", d[par], lin)
            if i >= 2:
                p1.wait("vector", d[2 + par], ln_odma[i - 2])
            ln_xfree.append(ln_tile(p1, l_x[par], l_o[par], par, 1e-5))
            p1.wait("scalar", sv, ln_xfree[i])
            ln_odma.append(dma(p1, "scalar", d[2 + par],
                               dst_dram[j * 128:(j + 1) * 128, :], l_o[par][:, :]))
            i += 1
    p1.emit(nc)
    c1.close()

    if stages <= 1:
        ctx.close()
        return nc
    kvctx = ExitStack()
    ekv = kvctx.enter_context
    k_fm = ekv(nc.sbuf_tensor("k_fm", [128, c.G * c.TB], BF16))
    v_tm = ekv(nc.sbuf_tensor("v_tm", [128, (c.TB // 128) * c.KV], BF16))
    p2 = Prog()
    phase_start(p2)
    c2 = ExitStack()
    e2 = c2.enter_context
    wk_s = e2(nc.sbuf_tensor("wk_s", [128, c.NCC * c.KV], BF16))
    wv_s = e2(nc.sbuf_tensor("wv_s", [128, c.NCC * c.KV], BF16))

    x_fm = [e2(nc.sbuf_tensor(f"x_fm{i}", [128, c.NCC * 512], BF16, is_transpose_dest=True)) for i in range(2)]
    cs_b = [e2(nc.sbuf_tensor("cs_b0", [128, 1024], F32))] * 2
    k_f32 = e2(nc.sbuf_tensor("k_f32", [128, 512], F32))
    k_sq = e2(nc.sbuf_tensor("k_sq", [128, 512], F32))
    k_row = e2(nc.sbuf_tensor("k_row", [1, 512], F32))
    k_tb = e2(nc.sbuf_tensor("k_tb", [128, 512], F32))

    with nc.psum_tensor("ps_v0", [128, c.KV], F32) as ps_v0, \
         nc.psum_tensor("ps_v1", [128, c.KV], F32) as ps_v1, \
         nc.psum_tensor("ps_k0", [128, 512], F32) as ps_k0, \
         nc.psum_tensor("ps_k1", [128, 512], F32) as ps_k1, \
         nc.psum_tensor("ps_ss2", [1, 512], F32) as ps_ss2, \
         nc.psum_tensor("ps_bc2", [128, 512], F32) as ps_bc2, \
         nc.psum_tensor("ps_sw2", [128, 512], F32) as ps_sw2:
        ps_v = (ps_v0, ps_v1)
        ps_k = (ps_k0, ps_k1)

        for wsrc, wdst in ((wkT, wk_s), (wvT, wv_s)):
            for cc in range(c.NCC):
                kvw = dma(p2, "gpsimd", d[4], wdst[:, cc * c.KV:(cc + 1) * c.KV],
                          wsrc[cc * 128:(cc + 1) * 128, :])

        NT2 = c.TB // 512
        vevac, kdone, xfm_free = [], [], []
        p2.wait("tensor", d[4], kvw)
        for tt in range(NT2):
            par = tt % 2
            if tt >= 2:
                p2.wait("sync", st, xfm_free[tt - 2])
            for cc in range(c.NCC):
                dma(p2, "sync", d[par], x_fm[par][:, cc * 512:(cc + 1) * 512],
                    ln1_b[tt * 512:(tt + 1) * 512, cc * 128:(cc + 1) * 128], transpose=True)
            if tt >= 1:
                p2.wait("sync", sv, kdone[(tt - 1) * c.G + c.G - 1])
            dma(p2, "sync", d[par], cs_b[par][:, 0:512], cos_b[:, tt * 512:(tt + 1) * 512])
            tile_dma = dma(p2, "sync", d[par], cs_b[par][:, 512:1024],
                           sin_b[:, tt * 512:(tt + 1) * 512])

            p2.wait("tensor", d[par], tile_dma)
            p2.wait("vector", d[par], tile_dma)
            for v4 in range(4):
                vp = v4 % 2
                gi = tt * 4 + v4
                if gi >= 2:
                    p2.wait("tensor", sv, vevac[gi - 2])
                for cc in range(c.NCC):
                    p2.add("tensor", lambda E, cc=cc, v4=v4, vp=vp, b=x_fm[par]: E.matmul(
                        ps_v[vp][:, :],
                        b[:, cc * 512 + v4 * 128: cc * 512 + (v4 + 1) * 128],
                        wv_s[:, cc * c.KV:(cc + 1) * c.KV],
                        start=(cc == 0), stop=(cc == c.NCC - 1)))
                vm = p2.seal_tensor_inc(st)
                p2.wait("vector", st, vm)
                p2.add("vector", lambda E, gi=gi, vp=vp: E.tensor_copy(
                    v_tm[:, gi * c.KV:(gi + 1) * c.KV], ps_v[vp][:, :]))
                vevac.append(p2.seal_vector_inc(sv))

            for g in range(c.G):
                kp = g % 2
                gi = tt * c.G + g
                if gi >= 2:
                    p2.wait("tensor", sv, kdone[gi - 2])
                for cc in range(c.NCC):
                    p2.add("tensor", lambda E, cc=cc, g=g, kp=kp, b=x_fm[par]: E.matmul(
                        ps_k[kp][:, :],
                        wk_s[:, cc * c.KV + g * 128: cc * c.KV + (g + 1) * 128],
                        b[:, cc * 512:(cc + 1) * 512],
                        start=(cc == 0), stop=(cc == c.NCC - 1)))
                kmm = p2.seal_tensor_inc(st)
                if g == c.G - 1:
                    xfm_free.append(kmm)

                p2.wait("vector", st, kmm)
                p2.add("vector", lambda E, kp=kp: E.tensor_copy(
                    k_f32[:, :], ps_k[kp][:, :]))
                p2.add("vector", lambda E: E.tensor_mul(k_sq[:, :], k_f32[:, :], k_f32[:, :]))
                ksq_v = p2.seal_vector_inc(sv)
                p2.wait("tensor", sv, ksq_v)
                p2.add("tensor", lambda E: E.matmul(ps_ss2[0:1, :], onesf[:, :], k_sq[:, :],
                                                    start=True, stop=True).then_inc(st.h, 1))
                ssm = st.inc()
                p2.wait("scalar", st, ssm)
                p2.add("scalar", lambda E: E.activation(
                    k_row[0:1, :], ps_ss2[0:1, :], AF.Sqrt, scale=float(1.0 / 128.0))
                    .then_inc(sa.h, 1))
                srt = sa.inc()
                p2.wait("vector", sa, srt)
                p2.add("vector", lambda E: E.tensor_scalar(
                    k_row[0:1, :], k_row[0:1, :], 1e-6, None, ALU.add))
                vpad(p2)
                p2.add("vector", lambda E: E.reciprocal(k_row[0:1, :], k_row[0:1, :]))
                rr = p2.seal_vector_inc(sv)
                p2.wait("tensor", sv, rr)
                p2.add("tensor", lambda E: E.matmul(ps_bc2[:, :], ones1[0:1, :], k_row[0:1, :],
                                                    start=True, stop=True).then_inc(st.h, 1))
                bcm = st.inc()
                p2.wait("vector", st, bcm)
                p2.add("vector", lambda E: E.scalar_tensor_tensor(
                    k_f32[:, :], k_f32[:, :], knw_s[:, 0:1], ps_bc2[:, :], ALU.mult, ALU.mult))
                kn_v = p2.seal_vector_inc(sv)
                p2.wait("tensor", sv, kn_v)
                p2.add("tensor", lambda E: E.matmul(ps_sw2[:, :], swp_s[:, :], k_f32[:, :],
                                                    start=True, stop=True).then_inc(st.h, 1))
                sw_t = st.inc()
                p2.wait("vector", st, sw_t)
                p2.wait("vector", sv, kn_v)
                p2.add("vector", lambda E, b=cs_b[par]: E.tensor_mul(
                    k_f32[:, :], k_f32[:, :], b[:, 0:512]))
                p2.add("vector", lambda E, b=cs_b[par]: E.tensor_mul(
                    k_tb[:, :], ps_sw2[:, :], b[:, 512:1024]))
                p2.add("vector", lambda E, g=g, tt=tt: E.tensor_add(
                    k_fm[0:64, g * c.TB + tt * 512: g * c.TB + (tt + 1) * 512],
                    k_f32[0:64, :], k_tb[0:64, :]))
                p2.add("vector", lambda E, g=g, tt=tt: E.tensor_sub(
                    k_fm[64:128, g * c.TB + tt * 512: g * c.TB + (tt + 1) * 512],
                    k_f32[64:128, :], k_tb[64:128, :]))
                kdone.append(p2.seal_vector_inc(sv))
    p2.emit(nc)
    c2.close()

    if stages <= 2:
        ctx.close()
        return nc
    p3 = Prog()
    phase_start(p3)
    c3 = ExitStack()
    e3 = c3.enter_context
    import os as _os
    QH = int(_os.environ.get('QH', 6))
    q_fm = e3(nc.sbuf_tensor("q_fm", [128, c.H * c.QT], BF16))
    lnofm = e3(nc.sbuf_tensor("lnofm", [128, c.NCC * 512], BF16, is_transpose_dest=True))
    wq_sl = [e3(nc.sbuf_tensor(f"wq_sl{i}", [128, QH * 128], BF16)) for i in range(2)]
    qf32 = e3(nc.sbuf_tensor("qf32", [128, QH * 512], F32))
    q_sq = e3(nc.sbuf_tensor("q_sq", [128, 512], F32))
    q_row = e3(nc.sbuf_tensor("q_row", [1, 512], F32))
    q_ta = e3(nc.sbuf_tensor("q_ta", [128, 512], F32))
    q_tb = e3(nc.sbuf_tensor("q_tb", [128, 512], F32))
    cs_o = e3(nc.sbuf_tensor("cs_o", [128, 1024], F32))
    t_sb = [e3(nc.sbuf_tensor(f"t_sb{i}", [128, c.QT], F32)) for i in range(2)]
    p_sb = [e3(nc.sbuf_tensor(f"p_sb{i}", [128, c.QT], BF16)) for i in range(2)]
    y_f32 = e3(nc.sbuf_tensor("y_f32", [128, c.QT], F32))
    y_out = [e3(nc.sbuf_tensor(f"y_out{i}", [128, c.QT], BF16)) for i in range(2)]
    z_row = e3(nc.sbuf_tensor("z_row", [1, c.QT], F32))

    hchunks = []
    h0 = 0
    while h0 < c.H:
        hchunks.append((h0, min(QH, c.H - h0)))
        h0 += QH

    ydma = []
    yio = []
    wq_guard = {0: 0, 1: 0}
    prev_m_tensor_gate = 0
    prev_m_rope_gate = 0
    lnofm_gate = 0
    for m in range(c.NM):
        p3.wait("sync", st, lnofm_gate)
        p3.wait("sync", sv, prev_m_rope_gate)
        for cc in (range(c.NCC) if "nolno" not in abl else []):
            dma(p3, "sync", d[6], lnofm[:, cc * 512:(cc + 1) * 512],
                ln1_o[m * c.QT:(m + 1) * c.QT, cc * 128:(cc + 1) * 128], transpose=True)
        if "nocs" not in abl:
            dma(p3, "sync", d[6], cs_o[:, 0:512], cos_o[:, m * c.QT:(m + 1) * c.QT])
            lnodma = dma(p3, "sync", d[6], cs_o[:, 512:1024], sin_o[:, m * c.QT:(m + 1) * c.QT])
        else:
            lnodma = 0

        rope_last = 0
        p3.wait("tensor", sv, prev_m_tensor_gate)
        p3.wait("tensor", d[6], lnodma)
        p3.wait("vector", d[6], lnodma)
        for ci, (h0, nh) in enumerate(hchunks if "noqb" not in abl else []):
            with nc.psum_tensor(f"ps_q_{m}_{ci}", [128, QH * 512], F32) as ps_q:
                p3.wait("tensor", sv, rope_last)
                qmm_done = []
                for cc in range(c.NCC):
                    spar = cc % 2
                    p3.wait("sync", st, wq_guard[spar])
                    sl = dma(p3, "sync", d[spar], wq_sl[spar][:, 0:nh * 128],
                             wqT_h[cc * 128:(cc + 1) * 128, h0 * 128:(h0 + nh) * 128])
                    p3.wait("tensor", d[spar], sl)
                    for hh in range(nh):
                        p3.add("tensor", lambda E, cc=cc, hh=hh, spar=spar: E.matmul(
                            ps_q[:, hh * 512:(hh + 1) * 512],
                            wq_sl[spar][:, hh * 128:(hh + 1) * 128],
                            lnofm[:, cc * 512:(cc + 1) * 512],
                            start=(cc == 0), stop=(cc == c.NCC - 1)))
                    qmm_done.append(p3.seal_tensor_inc(st))
                    wq_guard[spar] = qmm_done[-1]
                p3.wait("vector", st, qmm_done[-1])
                for hh in range(nh):
                    p3.add("vector", lambda E, hh=hh: E.tensor_copy(
                        qf32[:, hh * 512:(hh + 1) * 512], ps_q[:, hh * 512:(hh + 1) * 512]))
                qevac = p3.seal_vector_inc(sv)
                if dbg and m == c.NM - 1 and ci == 0:
                    p3.wait("scalar", sv, qevac)
                    dma(p3, "scalar", d[7], q_dbg[:, 0:nh * 512], qf32[:, 0:nh * 512])
            with nc.psum_tensor(f"ps_ss3_{m}_{ci}", [1, 512], F32) as ps_ss3, \
                 nc.psum_tensor(f"ps_bc3_{m}_{ci}", [128, 512], F32) as ps_bc3, \
                 nc.psum_tensor(f"ps_sw3_{m}_{ci}", [128, 512], F32) as ps_sw3:
                p3.wait("tensor", sv, qevac)
                p3.wait("vector", sv, qevac)
                for hh in (range(nh) if "nonorm" not in abl else []):
                    h = h0 + hh
                    p3.add("vector", lambda E, hh=hh: E.tensor_mul(
                        q_sq[:, :], qf32[:, hh * 512:(hh + 1) * 512],
                        qf32[:, hh * 512:(hh + 1) * 512]))
                    qsq_v = p3.seal_vector_inc(sv)
                    p3.wait("tensor", sv, qsq_v)
                    p3.add("tensor", lambda E: E.matmul(
                        ps_ss3[0:1, :], onesf[:, :], q_sq[:, :],
                        start=True, stop=True).then_inc(st.h, 1))
                    ssm = st.inc()
                    p3.wait("scalar", st, ssm)
                    p3.add("scalar", lambda E: E.activation(
                        q_row[0:1, :], ps_ss3[0:1, :], AF.Sqrt, scale=float(1.0 / 128.0))
                        .then_inc(sa.h, 1))
                    p3.wait("vector", sa, sa.inc())
                    p3.add("vector", lambda E: E.tensor_scalar(
                        q_row[0:1, :], q_row[0:1, :], 1e-6, None, ALU.add))
                    vpad(p3)
                    p3.add("vector", lambda E: E.reciprocal(q_row[0:1, :], q_row[0:1, :]))
                    rr = p3.seal_vector_inc(sv)
                    p3.wait("tensor", sv, rr)
                    p3.add("tensor", lambda E: E.matmul(
                        ps_bc3[:, :], ones1[0:1, :], q_row[0:1, :],
                        start=True, stop=True).then_inc(st.h, 1))
                    bcm = st.inc()
                    p3.wait("vector", st, bcm)
                    p3.add("vector", lambda E, hh=hh: E.scalar_tensor_tensor(
                        qf32[:, hh * 512:(hh + 1) * 512], qf32[:, hh * 512:(hh + 1) * 512],
                        qnw_s[:, 0:1], ps_bc3[:, :], ALU.mult, ALU.mult))
                    qn_v = p3.seal_vector_inc(sv)
                    p3.wait("tensor", sv, qn_v)
                    p3.add("tensor", lambda E, hh=hh: E.matmul(
                        ps_sw3[:, :], swp_s[:, :], qf32[:, hh * 512:(hh + 1) * 512],
                        start=True, stop=True).then_inc(st.h, 1))
                    swq_t = st.inc()
                    p3.wait("vector", st, swq_t)
                    p3.wait("vector", sv, qn_v)
                    p3.add("vector", lambda E, hh=hh: E.tensor_mul(
                        q_ta[:, :], qf32[:, hh * 512:(hh + 1) * 512], cs_o[:, 0:512]))
                    p3.add("vector", lambda E: E.tensor_mul(
                        q_tb[:, :], ps_sw3[:, :], cs_o[:, 512:1024]))
                    p3.add("vector", lambda E, h=h: E.tensor_add(
                        q_fm[0:64, h * c.QT:(h + 1) * c.QT], q_ta[0:64, :], q_tb[0:64, :]))
                    p3.add("vector", lambda E, h=h: E.tensor_sub(
                        q_fm[64:128, h * c.QT:(h + 1) * c.QT], q_ta[64:128, :], q_tb[64:128, :]))
                    rope_last = p3.seal_vector_inc(sv)
        prev_m_rope_gate = rope_last
        lnofm_gate_next = None

        NB = (m + 1) * c.SPAN
        with nc.psum_tensor(f"ps_s0_{m}", [128, c.QT], F32) as ps_s0, \
             nc.psum_tensor(f"ps_s1_{m}", [128, c.QT], F32) as ps_s1, \
             nc.psum_tensor(f"ps_y0_{m}", [128, c.QT], F32) as ps_y0, \
             nc.psum_tensor(f"ps_y1_{m}", [128, c.QT], F32) as ps_y1, \
             nc.psum_tensor(f"ps_z0_{m}", [1, c.QT], F32) as ps_z0, \
             nc.psum_tensor(f"ps_z1_{m}", [1, c.QT], F32) as ps_z1, \
             nc.psum_tensor(f"ps_yb0_{m}", [128, c.QT], F32) as ps_yb0, \
             nc.psum_tensor(f"ps_yb1_{m}", [128, c.QT], F32) as ps_yb1:
            ps_s = (ps_s0, ps_s1)
            ps_y = (ps_y0, ps_y1)
            ps_z = (ps_z0, ps_z1)
            ps_yb = (ps_yb0, ps_yb1)
            tanh_done, pvz_done = {}, {}
            m_yio = []
            hlist = list(range(c.H))
            for h in (hlist if "noattn" not in abl else []):
                g = h // (c.H // c.G)
                hp = h % 2
                qk_v = {}
                tanh_v = {}
                mask_v = {}
                exp_v = {}

                def lo_of(b):
                    return max(0, b - m * c.SPAN) * c.CW

                def emit_qk(b):
                    par = b % 2
                    lo = lo_of(b)
                    if (h, b - 2) in tanh_done:
                        p3.wait("tensor", sa, tanh_done[(h, b - 2)])
                    elif b < 2 and (h - 1, NB - 2 + b) in tanh_done:
                        p3.wait("tensor", sa, tanh_done[(h - 1, NB - 2 + b)])
                    if b == 0:
                        p3.wait("tensor", sv, rope_last)
                        if len(m_yio) >= 2:
                            p3.wait("tensor", sv, m_yio[-2])
                    p3.add("tensor", lambda E, g=g, h=h, b=b, par=par, lo=lo: E.matmul(
                        ps_s[par][:, lo:c.QT],
                        k_fm[:, g * c.TB + b * 128: g * c.TB + (b + 1) * 128],
                        q_fm[:, h * c.QT + lo:(h + 1) * c.QT],
                        start=True, stop=True).then_inc(st.h, 1))
                    qk_v[b] = st.inc()

                def emit_tanh(b):
                    par = b % 2
                    lo = lo_of(b)
                    k = b - m * c.SPAN
                    p3.wait("scalar", st, qk_v[b])
                    p3.add("scalar", lambda E, par=par, lo=lo: E.activation(
                        t_sb[par][:, lo:c.QT], ps_s[par][:, lo:c.QT],
                        AF.Tanh, scale=float(1.0 / (50.0 * math.sqrt(128.0))))
                        .then_inc(sa.h, 1))
                    tanh_done[(h, b)] = tanh_v[b] = sa.inc()
                    if k >= 0:
                        p3.wait("vector", sa, tanh_v[b])
                        p3.add("vector", lambda E, par=par, lo=lo: E.tensor_tensor(
                            t_sb[par][:, lo:lo + c.CW], t_sb[par][:, lo:lo + c.CW],
                            mask_s[:, :], ALU.min))
                        mask_v[b] = p3.seal_vector_inc(sv)

                def emit_exp(b):
                    par = b % 2
                    lo = lo_of(b)
                    if b in mask_v:
                        p3.wait("scalar", sv, mask_v[b])
                    if (h, b - 2) in pvz_done:
                        p3.wait("scalar", st, pvz_done[(h, b - 2)])
                    elif b < 2 and (h - 1, NB - 2 + b) in pvz_done:
                        p3.wait("scalar", st, pvz_done[(h - 1, NB - 2 + b)])
                    p3.add("scalar", lambda E, par=par, lo=lo: E.activation(
                        p_sb[par][:, lo:c.QT], t_sb[par][:, lo:c.QT], AF.Exp, scale=50.0)
                        .then_inc(sa.h, 1))
                    exp_v[b] = sa.inc()

                def emit_pvz(b):
                    par = b % 2
                    lo = lo_of(b)
                    p3.wait("tensor", sa, exp_v[b])
                    p3.add("tensor", lambda E, g=g, b=b, hp=hp, par=par, lo=lo, NB=NB: E.matmul(
                        ps_y[hp][:, lo:c.QT],
                        v_tm[:, b * c.KV + g * 128: b * c.KV + (g + 1) * 128],
                        p_sb[par][:, lo:c.QT],
                        start=(b == 0), stop=(b == NB - 1)))
                    p3.add("tensor", lambda E, b=b, hp=hp, par=par, lo=lo, NB=NB: E.matmul(
                        ps_z[hp][0:1, lo:c.QT],
                        ones_b[:, :], p_sb[par][:, lo:c.QT],
                        start=(b == 0), stop=(b == NB - 1)))
                    pvz_done[(h, b)] = p3.seal_tensor_inc(st)

                emit_qk(0)
                if NB > 1:
                    emit_qk(1)
                emit_tanh(0)
                for b in range(NB):
                    if b + 1 < NB:
                        emit_tanh(b + 1)
                    emit_exp(b)
                    if b + 2 < NB:
                        emit_qk(b + 2)
                    emit_pvz(b)
                p3.wait("vector", st, pvz_done[(h, NB - 1)])
                if len(ydma) >= 2:
                    p3.wait("vector", d[2 + (h % 2)], ydma[-2])
                p3.add("vector", lambda E, hp=hp: E.tensor_copy(
                    y_f32[:, :], ps_y[hp][:, :]))
                p3.add("vector", lambda E, hp=hp: E.reciprocal(
                    z_row[0:1, :], ps_z[hp][0:1, :]))
                zr = p3.seal_vector_inc(sv)
                p3.wait("tensor", sv, zr)
                p3.add("tensor", lambda E, hp=hp: E.matmul(
                    ps_yb[hp][:, :], ones1[0:1, :], z_row[0:1, :],
                    start=True, stop=True).then_inc(st.h, 1))
                ybm = st.inc()
                p3.wait("vector", st, ybm)
                p3.add("vector", lambda E, hp=hp: E.tensor_mul(
                    y_out[hp][:, :], y_f32[:, :], ps_yb[hp][:, :]))
                yv = p3.seal_vector_inc(sv)
                yio.append(yv)
                m_yio.append(yv)
                p3.wait("scalar", sv, yv)
                ydma.append(dma(p3, "scalar", d[2 + hp],
                                y_sc[:, h * c.TO + m * c.QT: h * c.TO + (m + 1) * c.QT],
                                y_out[hp][:, :]))
            prev_m_tensor_gate = m_yio[-1] if m_yio else prev_m_tensor_gate
            lnofm_gate = pvz_done[(hlist[-1], NB - 1)] if pvz_done else lnofm_gate
    p3.emit(nc)
    c3.close()
    kvctx.close()

    if stages <= 3:
        ctx.close()
        return nc
    p4 = Prog()
    phase_start(p4)
    c4 = ExitStack()
    e4 = c4.enter_context
    lnb["scr"] = e4(nc.sbuf_tensor("l_scr4", [128, c.C], F32))
    lnb["st"] = e4(nc.sbuf_tensor("l_st4", [128, 12], F32))
    wo_sb = e4(nc.sbuf_tensor("wo_sb", [128, c.H * c.C], BF16))
    bo_bc = e4(nc.sbuf_tensor("bo_bc", [128, c.C], F32))
    bo_row = e4(nc.sbuf_tensor("bo_row", [1, c.C], F32))
    bo_tmp = e4(nc.sbuf_tensor("bo_tmp", [1, 512], F32))
    ysb = [e4(nc.sbuf_tensor(f"ysb{i}", [128, c.H * 128], BF16)) for i in range(2)]
    x0_sb = [e4(nc.sbuf_tensor(f"x0_sb{i}", [128, c.C], F32)) for i in range(2)]
    x1_sb = [e4(nc.sbuf_tensor(f"x1_sb{i}", [128, c.C], F32)) for i in range(2)]
    l2_sb = [e4(nc.sbuf_tensor(f"l2_sb{i}", [128, c.C], BF16)) for i in range(2)]

    with nc.psum_tensor("ps_o0", [128, 512], F32) as ps_o0, \
         nc.psum_tensor("ps_o1", [128, 512], F32) as ps_o1, \
         nc.psum_tensor("ps_bb", [128, 512], F32) as ps_bb:
        ps_o = (ps_o0, ps_o1)
        for h in (range(c.H) if "nowold" not in abl and "nowo" not in abl else []):
            dma(p4, "sync", d[6], wo_sb[:, h * c.C:(h + 1) * c.C], woT_h[h * 128:(h + 1) * 128, :])
        wo_dma = dma(p4, "sync", d[6], bo_row[:, :], bo_t[:, :])
        p4.wait("vector", d[6], wo_dma)
        bo_done_v = 0
        for co in (range(c.NCO) if "nobo" not in abl and "nowo" not in abl else []):
            p4.add("vector", lambda E, co=co: E.tensor_copy(
                bo_tmp[0:1, :], bo_row[0:1, co * 512:(co + 1) * 512]))
            bcp = p4.seal_vector_inc(sv)
            p4.wait("tensor", sv, bcp)
            p4.add("tensor", lambda E: E.matmul(
                ps_bb[:, :], ones1[0:1, :], bo_tmp[0:1, :],
                start=True, stop=True).then_inc(st.h, 1))
            bm = st.inc()
            p4.wait("vector", st, bm)
            p4.add("vector", lambda E, co=co: E.tensor_copy(
                bo_bc[:, co * 512:(co + 1) * 512], ps_bb[:, :]))
            bo_done_v = p4.seal_vector_inc(sv)

        ev_flat = []
        tile_done = []
        x1dma = []
        for tt in (range(c.NTT) if "noloop" not in abl else []):
            par = tt % 2
            if tt >= 2:
                p4.wait("sync", sv, tile_done[tt - 2])
                p4.wait("vector", d[2 + par], x1dma[tt - 2])
            ysrc = y_sc[:, :].rearrange("p (h t) -> p h t", t=c.TO)[:, :, tt * 128:(tt + 1) * 128]
            for h0 in range(0, c.H, 4):
                dma(p4, "sync", d[par],
                    ysb[par][:, h0 * 128:(h0 + 4) * 128].rearrange("p (h t) -> p h t", t=128),
                    ysrc[:, h0:h0 + 4, :])
            ytile_dma = dma(p4, "sync", d[par], x0_sb[par][:, :], x_o[tt * 128:(tt + 1) * 128, :])

            p4.wait("vector", d[par], ytile_dma)
            p4.wait("vector", sv, bo_done_v)
            p4.add("vector", lambda E, b=x0_sb[par]: E.tensor_add(b[:, :], b[:, :], bo_bc[:, :]))
            x0b_v = p4.seal_vector_inc(sv)

            p4.wait("tensor", d[par], ytile_dma)
            for co in (range(c.NCO) if "nooproj" not in abl else []):
                op = (tt * c.NCO + co) % 2
                gi = tt * c.NCO + co
                if gi >= 2:
                    p4.wait("tensor", sv, ev_flat[gi - 2])
                for h in range(c.H):
                    p4.add("tensor", lambda E, h=h, co=co, op=op, b=ysb[par]: E.matmul(
                        ps_o[op][:, :],
                        b[:, h * 128:(h + 1) * 128],
                        wo_sb[:, h * c.C + co * 512: h * c.C + (co + 1) * 512],
                        start=(h == 0), stop=(h == c.H - 1)))
                omm = p4.seal_tensor_inc(st)
                p4.wait("vector", st, omm)
                p4.wait("vector", sv, x0b_v)
                p4.add("vector", lambda E, co=co, op=op, xb=x0_sb[par], xo=x1_sb[par]:
                       E.tensor_add(xo[:, co * 512:(co + 1) * 512],
                                    xb[:, co * 512:(co + 1) * 512],
                                    ps_o[op][:, :]))
                ev_flat.append(p4.seal_vector_inc(sv))
            if "nooproj" in abl:
                p4.add("vector", lambda E, xb=x0_sb[par], xo=x1_sb[par]: E.tensor_copy(xo[:, :], xb[:, :]))
                ev_flat.append(p4.seal_vector_inc(sv))
            if "noln2" not in abl:
                p4.wait("vector", sv, ev_flat[-1] if ev_flat else 0)
                ln2v = ln_tile(p4, x1_sb[par], l2_sb[par], par, 1e-5)
            else:
                p4.add("vector", lambda E, b=l2_sb[par], xb=x1_sb[par]: E.tensor_copy(b[:, :], xb[:, 0:c.C]))
                ln2v = p4.seal_vector_inc(sv)
            tile_done.append(ln2v)
            p4.wait("scalar", sv, ln2v)
            dma(p4, "scalar", d[2 + par], x1_o[tt * 128:(tt + 1) * 128, :], x1_sb[par][:, :])
            x1dma.append(dma(p4, "scalar", d[2 + par], ln2_o[tt * 128:(tt + 1) * 128, :], l2_sb[par][:, :]))
    p4.emit(nc)
    c4.close()

    if stages <= 4:
        ctx.close()
        return nc
    p5 = Prog()
    phase_start(p5)
    c5 = ExitStack()
    e5 = c5.enter_context
    l2fm = e5(nc.sbuf_tensor("l2fm", [128, c.NCC * c.TO], BF16, is_transpose_dest=True))
    g_sl = [e5(nc.sbuf_tensor(f"g_sl{i}", [128, c.NCC * 512], BF16, is_transpose_dest=True)) for i in range(2)]
    u_sl = [e5(nc.sbuf_tensor(f"u_sl{i}", [128, c.NCC * 512], BF16, is_transpose_dest=True)) for i in range(2)]
    sg_t = [e5(nc.sbuf_tensor(f"sg_t{i}", [128, 512], F32)) for i in range(2)]
    h_t = [e5(nc.sbuf_tensor(f"h_t{i}", [128, 512], BF16)) for i in range(2)]
    h_sq = [e5(nc.sbuf_tensor(f"h_sq{i}", [128, 512], F32)) for i in range(2)]
    s2row = e5(nc.sbuf_tensor("s2row", [1, c.TO], F32))
    NQ4 = c.TO // 512

    with nc.psum_tensor("ps_g0", [128, 512], F32) as ps_g0, \
         nc.psum_tensor("ps_g1", [128, 512], F32) as ps_g1, \
         nc.psum_tensor("ps_u0", [128, 512], F32) as ps_u0, \
         nc.psum_tensor("ps_u1", [128, 512], F32) as ps_u1, \
         nc.psum_tensor("ps_s2", [1, NQ4 * 512], F32) as ps_s2:
        ps_g = (ps_g0, ps_g1)
        ps_u = (ps_u0, ps_u1)
        for cc in range(c.NCC):
            l2_dma = dma(p5, "sync", d[6], l2fm[:, cc * c.TO:(cc + 1) * c.TO],
                         ln2_o[:, cc * 128:(cc + 1) * 128], transpose=True)

        h_done, hdma, s2mm_l = [], [], []
        s2q = []
        s2mm_by_gi = {}
        gu_mm_last = {}
        s2mm = 0
        for iu4 in range(c.NU // 4):
            par = iu4 % 2
            if iu4 >= 2:
                p5.wait("sync", st, gu_mm_last[iu4 - 2])
            slab_last = 0
            for base, buf in ((iu4 * 4 * 128, g_sl[par]), (c.U + iu4 * 4 * 128, u_sl[par])):
                for cc in range(c.NCC):
                    slab_last = dma(p5, "sync", d[par], buf[:, cc * 512:(cc + 1) * 512],
                                    wgu_q[base:base + 512, cc * 128:(cc + 1) * 128],
                                    transpose=True)
            p5.wait("tensor", d[par], slab_last)
            if iu4 == 0:
                p5.wait("tensor", d[6], l2_dma)
            for sub in range(4):
                iu = iu4 * 4 + sub
                for tq in range(NQ4):
                    gi = iu * NQ4 + tq
                    gp = gi % 2
                    if gi >= 2:
                        p5.wait("tensor", sv, h_done[gi - 2])
                    for cc in range(c.NCC):
                        p5.add("tensor", lambda E, cc=cc, sub=sub, tq=tq, gp=gp, b=g_sl[par]:
                               E.matmul(ps_g[gp][:, :],
                                        b[:, cc * 512 + sub * 128: cc * 512 + (sub + 1) * 128],
                                        l2fm[:, cc * c.TO + tq * 512: cc * c.TO + (tq + 1) * 512],
                                        start=(cc == 0), stop=(cc == c.NCC - 1)))
                    for cc in range(c.NCC):
                        p5.add("tensor", lambda E, cc=cc, sub=sub, tq=tq, gp=gp, b=u_sl[par]:
                               E.matmul(ps_u[gp][:, :],
                                        b[:, cc * 512 + sub * 128: cc * 512 + (sub + 1) * 128],
                                        l2fm[:, cc * c.TO + tq * 512: cc * c.TO + (tq + 1) * 512],
                                        start=(cc == 0), stop=(cc == c.NCC - 1)))
                    gu = p5.seal_tensor_inc(st)
                    gu_mm_last[iu4] = gu
                    p5.wait("scalar", st, gu)
                    if gi >= 2:
                        p5.wait("scalar", sv, h_done[gi - 2])
                    p5.add("scalar", lambda E, gp=gp, sg=sg_t[gi % 2]: E.activation(
                        sg[:, :], ps_g[gp][:, :], AF.Silu)
                        .then_inc(sa.h, 1))
                    silu_v = sa.inc()
                    p5.wait("vector", sa, silu_v)
                    if gi >= 2:
                        p5.wait("vector", d[2 + (gi % 2)], hdma[gi - 2])
                        if (gi - 2) in s2mm_by_gi:
                            p5.wait("vector", st, s2mm_by_gi[gi - 2])
                    hb = h_t[gi % 2]
                    p5.add("vector", lambda E, gp=gp, hb=hb, sg=sg_t[gi % 2]: E.tensor_mul(
                        hb[:, :], sg[:, :], ps_u[gp][:, :]))
                    p5.add("vector", lambda E, hb=hb, hq=h_sq[gi % 2]: E.tensor_mul(
                        hq[:, :], hb[:, :], hb[:, :]))
                    h_done.append(p5.seal_vector_inc(sv))
                    s2q.append((gi, tq, iu))
                    if len(s2q) >= 2:
                        g2, t2_, i2 = s2q.pop(0)
                        p5.wait("tensor", sv, h_done[g2])
                        p5.add("tensor", lambda E, tq=t2_, iu=i2, hq=h_sq[g2 % 2]: E.matmul(
                            ps_s2[0:1, tq * 512:(tq + 1) * 512], onesf[:, :], hq[:, :],
                            start=(iu == 0), stop=(iu == c.NU - 1)).then_inc(st.h, 1))
                        s2mm = st.inc()
                        s2mm_l.append(s2mm)
                        s2mm_by_gi[g2] = s2mm
                    p5.wait("scalar", sv, h_done[-1])
                    hdma.append(dma(p5, "scalar", d[2 + (gi % 2)],
                                    h_sc[iu * 128:(iu + 1) * 128, tq * 512:(tq + 1) * 512],
                                    hb[:, :]))
        while s2q:
            g2, t2_, i2 = s2q.pop(0)
            p5.wait("tensor", sv, h_done[g2])
            p5.add("tensor", lambda E, tq=t2_, iu=i2, hq=h_sq[g2 % 2]: E.matmul(
                ps_s2[0:1, tq * 512:(tq + 1) * 512], onesf[:, :], hq[:, :],
                start=(iu == 0), stop=(iu == c.NU - 1)).then_inc(st.h, 1))
            s2mm = st.inc()
            s2mm_l.append(s2mm)
            s2mm_by_gi[g2] = s2mm
        p5.wait("scalar", st, s2mm)
        p5.add("scalar", lambda E: E.activation(
            s2row[0:1, :], ps_s2[0:1, 0:c.TO], AF.Sqrt, bias=eps_ffn[0:1, 0:1],
            scale=float(1.0 / c.U)).then_inc(sa.h, 1))
        p5.wait("vector", sa, sa.inc())
        vpad(p5)
        p5.add("vector", lambda E: E.reciprocal(s2row[0:1, :], s2row[0:1, :]))
        iv = p5.seal_vector_inc(sv)
        p5.wait("scalar", sv, iv)
        dma(p5, "scalar", d[7], inv_sc[:, :], s2row[0:1, :])
    p5.emit(nc)
    c5.close()

    if stages <= 5:
        ctx.close()
        return nc
    p6 = Prog()
    phase_start(p6)
    c6 = ExitStack()
    e6 = c6.enter_context
    wp_sl = e6(nc.sbuf_tensor("wp_sl", [128, c.NU * 512], BF16, is_transpose_dest=True))
    h_sl = [e6(nc.sbuf_tensor(f"h_sl{i}", [128, c.NU * 128], BF16)) for i in range(2)]
    invcol = e6(nc.sbuf_tensor("invcol", [128, c.NTT], F32))
    x1r = [e6(nc.sbuf_tensor(f"x1r{i}", [128, 512], F32)) for i in range(2)]
    o_sb = [e6(nc.sbuf_tensor(f"o_sb{i}", [128, 512], F32)) for i in range(2)]

    with nc.psum_tensor("ps_f0", [128, 512], F32) as ps_f0, \
         nc.psum_tensor("ps_f1", [128, 512], F32) as ps_f1:
        ps_f = (ps_f0, ps_f1)
        for tt in range(c.NTT):
            inv_dma = dma(p6, "sync", d[6], invcol[:, tt:tt + 1],
                          inv_sc[0:1, tt * 128:(tt + 1) * 128].rearrange("o p -> p o"))
        fmm_done, fev_done, odma = [], [], []
        co_mm_last = 0
        for co in range(c.NCO):
            if co >= 1:
                p6.wait("sync", st, co_mm_last)
            for iu in range(c.NU):
                wp_dma = dma(p6, "sync", d[7], wp_sl[:, iu * 512:(iu + 1) * 512],
                             wp_q[co * 512:(co + 1) * 512, iu * 128:(iu + 1) * 128],
                             transpose=True)
            for tt in range(c.NTT):
                par = tt % 2
                gi = co * c.NTT + tt
                if gi >= 2:
                    p6.wait("sync", st, fmm_done[gi - 2])
                hsrc = h_sc[:, tt * 128:(tt + 1) * 128].rearrange(
                    "(a p) t -> p a t", p=128)
                for a0 in range(0, c.NU, 8):
                    dma(p6, "sync", d[par],
                        h_sl[par][:, a0 * 128:(a0 + 8) * 128].rearrange(
                            "p (a t) -> p a t", t=128),
                        hsrc[:, a0:a0 + 8, :])
                hx_dma = dma(p6, "sync", d[par], x1r[par][:, :],
                             x1_o[tt * 128:(tt + 1) * 128, co * 512:(co + 1) * 512])

                p6.wait("tensor", d[par], hx_dma)
                p6.wait("tensor", d[7], wp_dma)
                if gi >= 2:
                    p6.wait("tensor", sv, fev_done[gi - 2])
                fp = gi % 2
                for iu in range(c.NU):
                    p6.add("tensor", lambda E, iu=iu, fp=fp, b=h_sl[par]: E.matmul(
                        ps_f[fp][:, :],
                        b[:, iu * 128:(iu + 1) * 128],
                        wp_sl[:, iu * 512:(iu + 1) * 512],
                        start=(iu == 0), stop=(iu == c.NU - 1)))
                fmm_done.append(p6.seal_tensor_inc(st))
                p6.wait("vector", st, fmm_done[-1])
                p6.wait("vector", d[par], hx_dma)
                p6.wait("vector", d[6], inv_dma)
                if gi >= 2:
                    p6.wait("vector", d[2 + par], odma[gi - 2])
                p6.add("vector", lambda E, tt=tt, fp=fp, par=par: E.scalar_tensor_tensor(
                    o_sb[par][:, :], ps_f[fp][:, :], invcol[:, tt:tt + 1],
                    x1r[par][:, :], ALU.mult, ALU.add))
                fev_done.append(p6.seal_vector_inc(sv))
                p6.wait("scalar", sv, fev_done[-1])
                odma.append(dma(p6, "scalar", d[2 + par],
                                out_t[tt * 128:(tt + 1) * 128, co * 512:(co + 1) * 512],
                                o_sb[par][:, :]))
            co_mm_last = fmm_done[-1]
        for eng in ENGS:
            p6.wait(eng, d[2], d[2].n)
            p6.wait(eng, d[3], d[3].n)
    p6.emit(nc)
    c6.close()

    ctx.close()
    return nc


def _rope_tables(T, D=128):
    ar = np.arange(0, D, 2, dtype=np.float32)
    inv_freq = (np.float32(1.0) /
                np.power(np.float32(10000.0), ar / np.float32(D))).astype(np.float32)
    t = np.arange(T, dtype=np.float32)
    freqs = np.outer(t, inv_freq).astype(np.float32)
    return np.cos(freqs).astype(np.float32), np.sin(freqs).astype(np.float32)


def make_inputs(cfg, core, x, Wq, Wk, Wv, Wo, bo, qnw, knw, Wgu, Wp):
    c = cfg
    b, p = core // 4, core % 4
    cosT, sinT = _rope_tables(c.TB, 128)
    cosD = np.concatenate([cosT, cosT], axis=1)
    sinD = np.concatenate([sinT, sinT], axis=1)
    jj = np.arange(128)[:, None]
    ii = np.arange(c.CW)[None, :]
    mask = np.where(p + c.S * ii >= jj, np.float32(1e30), np.float32(-1.0)).astype(np.float32)
    swp = np.zeros((128, 128), np.float32)
    for d in range(64):
        swp[64 + d, d] = 1.0
        swp[d, 64 + d] = 1.0
    return {
        "x_b": np.ascontiguousarray(x[b]),
        "x_o": np.ascontiguousarray(x[b, p::c.S]),
        "wqT": np.ascontiguousarray(Wq.T),
        "wkT": np.ascontiguousarray(Wk.T),
        "wvT": np.ascontiguousarray(Wv.T),
        "woT": np.ascontiguousarray(Wo.T),
        "bo": np.ascontiguousarray(bo.reshape(1, -1)),
        "qnw": np.ascontiguousarray(qnw.reshape(128, 1)),
        "knw": np.ascontiguousarray(knw.reshape(128, 1)),
        "cos_b": np.ascontiguousarray(cosD.T),
        "sin_b": np.ascontiguousarray(sinD.T),
        "cos_o": np.ascontiguousarray(cosD[p::c.S].T),
        "sin_o": np.ascontiguousarray(sinD[p::c.S].T),
        "mask": mask,
        "swp": swp,
        "wgu": np.ascontiguousarray(Wgu),
        "wp": np.ascontiguousarray(Wp),
    }


def kernel(L0_attn_x, L0_attn_Wq, L0_attn_Wk, L0_attn_Wv, L0_attn_Wo, L0_attn_bo,
           L0_attn_q_norm_w, L0_attn_k_norm_w, L0_ffn_w_gate_up, L0_ffn_w_proj):
    cfg = Cfg()
    x = np.asarray(L0_attn_x, dtype=np.float32)
    nc = build_nc(cfg)
    in_maps = [make_inputs(cfg, core, x, np.asarray(L0_attn_Wq), np.asarray(L0_attn_Wk),
                           np.asarray(L0_attn_Wv), np.asarray(L0_attn_Wo),
                           np.asarray(L0_attn_bo), np.asarray(L0_attn_q_norm_w),
                           np.asarray(L0_attn_k_norm_w), np.asarray(L0_ffn_w_gate_up),
                           np.asarray(L0_ffn_w_proj))
               for core in range(8)]
    res = run_bass_kernel_spmd(nc, in_maps, core_ids=list(range(8)))
    out = np.empty((2, cfg.TB, cfg.C), dtype=np.float32)
    for core in range(8):
        b, p = core // 4, core % 4
        out[b, p::cfg.S, :] = res.results[core]["out"]
    return out


# version 07 (final)
# speedup vs baseline: 1.5701x; 1.0495x over previous
import math
from contextlib import ExitStack

import numpy as np
import ml_dtypes
import concourse.bass as bass
import concourse.mybir as mybir
from concourse.bass_utils import run_bass_kernel_spmd

bf16 = ml_dtypes.bfloat16
F32 = mybir.dt.float32
BF16 = mybir.dt.bfloat16
AF = mybir.ActivationFunctionType
ALU = mybir.AluOpType
AX = mybir.AxisListType

F32EPS = float(np.finfo(np.float32).eps)
ENGS = ("sync", "vector", "scalar", "tensor", "gpsimd")


class Cfg:
    def __init__(self, TB=8192, C=2048, H=16, G=4, U=8192):
        self.S = 4
        self.TB = TB
        self.TO = TB // self.S
        self.C = C
        self.H = H
        self.G = G
        self.KV = G * 128
        self.U = U
        self.QT = 512
        self.KB = 128
        self.CW = self.KB // self.S
        self.QG = 64
        self.NCC = C // 128
        self.NM = self.TO // self.QT
        self.SPAN = self.S * self.QT // self.KB
        self.NU = U // 128
        self.NCO = C // 512
        self.NTT = self.TO // 128
        assert self.H * 128 == C and self.TO % self.QT == 0
        assert C % 512 == 0 and U % 512 == 0 and TB % 512 == 0
        assert self.SPAN % 2 == 0


class Sem:
    def __init__(self, handle):
        self.h = handle
        self.n = 0

    def inc(self, d=1):
        self.n += d
        return self.n


class Prog:
    def __init__(self):
        self.q = {k: [] for k in ENGS}

    def add(self, eng, fn):
        self.q[eng].append(fn)

    def wait(self, eng, sem, val):
        if val > 0:
            f = lambda E, s=sem.h, v=val: E.wait_ge(s, v)
            f.is_wait = True
            self.q[eng].append(f)

    def seal_tensor_inc(self, sem):
        f = self.q["tensor"][-1]
        v = sem.inc()
        self.q["tensor"][-1] = lambda E, f=f, s=sem.h: f(E).then_inc(s, 1)
        return v

    def seal_vector_inc(self, sem):
        f = self.q["vector"][-1]
        v = sem.inc()
        self.q["vector"][-1] = lambda E, f=f, s=sem.h: f(E).then_inc(s, 1)
        return v

    def emit(self, nc):
        with nc.Block() as block:
            for eng in ENGS:
                ops = self.q[eng]
                if not ops or all(getattr(f, "is_wait", False) for f in ops):
                    continue

                def body(E, ops=ops):
                    for f in ops:
                        f(E)

                getattr(block, eng)(body)


def build_nc(cfg: Cfg, dbg=False, stages=9, abl=()):
    c = cfg
    nc = bass.Bass("TRN2", target_bir_lowering=False, detect_race_conditions=False)

    x_b = nc.dram_tensor("x_b", [c.TB, c.C], F32, kind="ExternalInput")
    x_o = nc.dram_tensor("x_o", [c.TO, c.C], F32, kind="ExternalInput")
    wqT = nc.dram_tensor("wqT", [c.C, c.C], F32, kind="ExternalInput")
    wkT = nc.dram_tensor("wkT", [c.C, c.KV], F32, kind="ExternalInput")
    wvT = nc.dram_tensor("wvT", [c.C, c.KV], F32, kind="ExternalInput")
    woT = nc.dram_tensor("woT", [c.C, c.C], F32, kind="ExternalInput")
    bo_t = nc.dram_tensor("bo", [1, c.C], F32, kind="ExternalInput")
    qnw_t = nc.dram_tensor("qnw", [128, 1], F32, kind="ExternalInput")
    knw_t = nc.dram_tensor("knw", [128, 1], F32, kind="ExternalInput")
    cos_b = nc.dram_tensor("cos_b", [128, c.TB], F32, kind="ExternalInput")
    sin_b = nc.dram_tensor("sin_b", [128, c.TB], F32, kind="ExternalInput")
    cos_o = nc.dram_tensor("cos_o", [128, c.TO], F32, kind="ExternalInput")
    sin_o = nc.dram_tensor("sin_o", [128, c.TO], F32, kind="ExternalInput")
    mask_t = nc.dram_tensor("mask", [128, c.CW], F32, kind="ExternalInput")
    swp_t = nc.dram_tensor("swp", [128, 128], F32, kind="ExternalInput")
    wgu = nc.dram_tensor("wgu", [2 * c.U, c.C], F32, kind="ExternalInput")
    wp = nc.dram_tensor("wp", [c.C, c.U], F32, kind="ExternalInput")
    out_t = nc.dram_tensor("out", [c.TO, c.C], F32, kind="ExternalOutput")

    kk = dict(kind="ExternalOutput") if dbg else {}
    ln1_b = nc.dram_tensor("ln1_b", [c.TB, c.C], BF16, **kk)
    ln1_o = nc.dram_tensor("ln1_o", [c.TO, c.C], BF16, **kk)
    ln2_o = nc.dram_tensor("ln2_o", [c.TO, c.C], BF16, **kk)
    x1_o = nc.dram_tensor("x1_o", [c.TO, c.C], F32, **kk)
    wgu_q = nc.dram_tensor("wgu_q", [2 * c.U, c.C], BF16, **kk)
    wp_q = nc.dram_tensor("wp_q", [c.C, c.U], BF16, **kk)
    wqT_h = nc.dram_tensor("wqT_h", [c.C, c.C], BF16, **kk)
    woT_h = nc.dram_tensor("woT_h", [c.C, c.C], BF16, **kk)
    h_sc = nc.dram_tensor("h_sc", [c.U, c.TO], BF16, **kk)
    y_sc = nc.dram_tensor("y_sc", [128, c.H * c.TO], BF16, **kk)
    inv_sc = nc.dram_tensor("inv_sc", [1, c.TO], F32, **kk)
    q_dbg = nc.dram_tensor("q_dbg", [128, 4096], F32, **kk) if dbg else None

    ctx = ExitStack()
    e = ctx.enter_context

    ones_b = e(nc.sbuf_tensor("ones_b", [128, 1], BF16))
    ones1 = e(nc.sbuf_tensor("ones1", [1, 128], F32))
    onesf = e(nc.sbuf_tensor("onesf", [128, 1], F32))
    qnw_s = e(nc.sbuf_tensor("qnw_s", [128, 1], F32))
    knw_s = e(nc.sbuf_tensor("knw_s", [128, 1], F32))
    mask_s = e(nc.sbuf_tensor("mask_s", [128, c.CW], F32))
    swp_s = e(nc.sbuf_tensor("swp_s", [128, 128], F32))
    lnb = {}
    eps_ln = e(nc.sbuf_tensor("eps_ln", [128, 1], F32))
    padt = e(nc.sbuf_tensor("padt", [128, 8], F32))
    padt2 = e(nc.sbuf_tensor("padt2", [128, 8], F32))
    invC = e(nc.sbuf_tensor("invC", [128, 1], F32))
    eps_ffn = e(nc.sbuf_tensor("eps_ffn", [128, 1], F32))


    d = [Sem(e(nc.semaphore(f"d{i}"))) for i in range(12)]
    sv = Sem(e(nc.semaphore("sv")))
    sp = Sem(e(nc.semaphore("sp")))
    sa = Sem(e(nc.semaphore("sa")))
    st = Sem(e(nc.semaphore("st")))
    ALLSEMS = tuple(d) + (sv, sp, sa, st)

    def phase_start(p: Prog):
        for eng in ENGS:
            for s in ALLSEMS:
                p.wait(eng, s, s.n)

    def vpad(p, n=5):
        for _ in range(n):
            p.add("vector", lambda E: E.tensor_copy(padt[:, :], padt2[:, :]))

    def dma(p, eng, sem, out, in_, transpose=False):
        if transpose:
            p.add(eng, lambda E, o=out, i=in_, s=sem.h:
                  E.dma_start_transpose(o, i).then_inc(s, 16))
        else:
            p.add(eng, lambda E, o=out, i=in_, s=sem.h:
                  E.dma_start(out=o, in_=i).then_inc(s, 16))
        return sem.inc(16)

    def make_quant_bufs(ee, tag):
        QW = 2048
        return dict(
            win=[ee(nc.sbuf_tensor(f"{tag}win{i}", [128, QW], F32)) for i in range(2)],
            wb=ee(nc.sbuf_tensor(f"{tag}wb", [128, QW], BF16)),
            sum=ee(nc.sbuf_tensor(f"{tag}sum", [128, QW // c.QG], F32)),
            scl=ee(nc.sbuf_tensor(f"{tag}scl", [128, QW // c.QG], BF16)),
            rcp=ee(nc.sbuf_tensor(f"{tag}rcp", [128, QW // c.QG], F32)),
            sclf=ee(nc.sbuf_tensor(f"{tag}sclf", [128, QW // c.QG], F32)),
            qq=ee(nc.sbuf_tensor(f"{tag}qq", [128, QW], BF16)),
            wh=ee(nc.sbuf_tensor(f"{tag}wh", [128, QW], BF16)),
            out=[ee(nc.sbuf_tensor(f"{tag}out{i}", [128, QW], BF16)) for i in range(2)],
        )

    def make_quant_emitter(p, bufs, tiles):
        state = {"i": 0, "winfree": [], "outrdy": [], "outdma": []}

        def step(n=1):
            for _ in range(n):
                i = state["i"]
                if i >= len(tiles):
                    return
                state["i"] += 1
                srct, dstt, r0, c0, w = tiles[i]
                g = w // c.QG
                ld, sr = d[8 + (i % 2)], d[10 + (i % 2)]
                if len(state["winfree"]) >= 2:
                    p.wait("sync", sv, state["winfree"][-2])
                qin = dma(p, "sync", ld, bufs["win"][i % 2][:, 0:w],
                          srct[r0:r0 + 128, c0:c0 + w])
                p.wait("vector", ld, qin)
                if len(state["outdma"]) >= 2:
                    p.wait("vector", sr, state["outdma"][-2])
                wi, ob = bufs["win"][i % 2], bufs["out"][i % 2]
                p.add("vector", lambda E, wi=wi, w=w: E.tensor_copy(bufs["wb"][:, 0:w], wi[:, 0:w]))
                state["winfree"].append(p.seal_vector_inc(sv))
                p.wait("vector", sv, state["winfree"][-1])
                p.add("vector", lambda E, w=w, g=g: E.tensor_reduce(
                    bufs["sum"][:, 0:g], bufs["wb"][:, 0:w].rearrange("p (g x) -> p g x", x=c.QG),
                    AX.X, ALU.add, apply_absolute_value=True))
                vpad(p)
                p.add("vector", lambda E, g=g: E.tensor_scalar(
                    bufs["sclf"][:, 0:g], bufs["sum"][:, 0:g], 1.0 / c.QG, 1e-8, ALU.mult, ALU.max))
                vpad(p)
                p.add("vector", lambda E, g=g: E.tensor_copy(bufs["scl"][:, 0:g], bufs["sclf"][:, 0:g]))
                vpad(p)
                p.add("vector", lambda E, g=g: E.tensor_copy(bufs["sclf"][:, 0:g], bufs["scl"][:, 0:g]))
                vpad(p)
                p.add("vector", lambda E, g=g: E.reciprocal(bufs["rcp"][:, 0:g], bufs["sclf"][:, 0:g]))
                vpad(p)
                p.add("vector", lambda E, w=w, g=g: E.tensor_tensor(
                    bufs["qq"][:, 0:w].rearrange("p (g x) -> p g x", x=c.QG),
                    bufs["wb"][:, 0:w].rearrange("p (g x) -> p g x", x=c.QG),
                    bufs["rcp"][:, 0:g].broadcast_to([128, g, c.QG]), ALU.mult))
                p.add("vector", lambda E, w=w: E.tensor_scalar(
                    bufs["qq"][:, 0:w], bufs["qq"][:, 0:w], 192.0, None, ALU.add))
                p.add("vector", lambda E, w=w: E.tensor_scalar(
                    bufs["qq"][:, 0:w], bufs["qq"][:, 0:w], 192.0, 1.0, ALU.subtract, ALU.min))
                p.add("vector", lambda E, w=w: E.tensor_scalar(
                    bufs["qq"][:, 0:w], bufs["qq"][:, 0:w], -1.0, None, ALU.max))
                p.add("vector", lambda E, w=w, g=g: E.tensor_tensor(
                    bufs["wh"][:, 0:w].rearrange("p (g x) -> p g x", x=c.QG),
                    bufs["qq"][:, 0:w].rearrange("p (g x) -> p g x", x=c.QG),
                    bufs["sclf"][:, 0:g].broadcast_to([128, g, c.QG]), ALU.mult))
                p.add("vector", lambda E, w=w: E.tensor_sub(
                    bufs["wh"][:, 0:w], bufs["wh"][:, 0:w], bufs["wb"][:, 0:w]))
                p.add("vector", lambda E, ob=ob, w=w: E.tensor_add(
                    ob[:, 0:w], bufs["wb"][:, 0:w], bufs["wh"][:, 0:w]))
                state["outrdy"].append(p.seal_vector_inc(sv))
                p.wait("scalar", sv, state["outrdy"][-1])
                state["outdma"].append(dma(p, "scalar", sr,
                                           dstt[r0:r0 + 128, c0:c0 + w], ob[:, 0:w]))

        return step

    pQ = Prog()
    qctx = ExitStack()
    eq = qctx.enter_context
    QW = 2048
    qtiles = []
    for r0 in range(0, 2 * c.U, 128):
        for c0 in range(0, c.C, QW):
            qtiles.append((wgu, wgu_q, r0, c0, min(QW, c.C - c0)))
    wptiles = []
    for r0 in range(0, c.C, 128):
        for c0 in range(0, c.U, QW):
            wptiles.append((wp, wp_q, r0, c0, min(QW, c.U - c0)))
    nq_half = len(qtiles) // 2

    qbufsA = make_quant_bufs(eq, "qA")

    dma(pQ, "sync", d[6], qnw_s[:, :], qnw_t[:, :])
    dma(pQ, "sync", d[6], knw_s[:, :], knw_t[:, :])
    dma(pQ, "sync", d[6], mask_s[:, :], mask_t[:, :])
    dma(pQ, "sync", d[6], swp_s[:, :], swp_t[:, :])
    pQ.add("vector", lambda E: E.memset(ones_b[:, :], 1.0))
    pQ.add("vector", lambda E: E.memset(ones1[:, :], 1.0))
    pQ.add("vector", lambda E: E.memset(onesf[:, :], 1.0))
    pQ.add("vector", lambda E: E.memset(eps_ln[:, :], 1e-5))
    pQ.add("vector", lambda E: E.memset(eps_ffn[:, :], F32EPS))
    pQ.add("vector", lambda E: E.memset(invC[:, :], 1.0 / c.C))
    pQ.add("vector", lambda E: E.memset(padt2[:, :], 0.0))

    c_wq = [eq(nc.sbuf_tensor(f"c_wq{i}", [128, c.C], BF16)) for i in range(2)]
    qstepA = make_quant_emitter(pQ, qbufsA, qtiles)
    qstepA(nq_half)

    ctiles = [(wqT, wqT_h, r0) for r0 in range(0, c.C, 128)] + \
             [(woT, woT_h, r0) for r0 in range(0, c.C, 128)]
    cw_store = []
    for i, (srct, dstt, r0) in enumerate(ctiles):
        ds = d[4 + (i % 2)]
        if i >= 2:
            pQ.wait("gpsimd", ds, cw_store[i - 2])
        v = dma(pQ, "gpsimd", ds, c_wq[i % 2][:, :], srct[r0:r0 + 128, :])
        pQ.wait("gpsimd", ds, v)
        cw_store.append(dma(pQ, "gpsimd", ds, dstt[r0:r0 + 128, :], c_wq[i % 2][:, :]))
    pQ.emit(nc)

    p1 = Prog()
    phase_start(p1)
    c1 = ExitStack()
    e1 = c1.enter_context
    lnb["scr"] = e1(nc.sbuf_tensor("l_scr1", [128, c.C], F32))
    lnb["st"] = e1(nc.sbuf_tensor("l_st1", [128, 12], F32))
    l_x = [e1(nc.sbuf_tensor(f"l_x{i}", [128, c.C], F32)) for i in range(2)]
    l_o = [e1(nc.sbuf_tensor(f"l_o{i}", [128, c.C], BF16)) for i in range(2)]

    def ln_tile(p: Prog, xbuf, obuf, par, eps):
        o = 6 * par
        l_scr, l_st = lnb["scr"], lnb["st"]
        p.add("vector", lambda E: E.reduce_sum(
            l_st[:, o + 0:o + 1], xbuf[:, :], AX.X))
        p.add("vector", lambda E: E.tensor_mul(l_scr[:, :], xbuf[:, :], xbuf[:, :]))
        p.add("vector", lambda E: E.reduce_sum(
            l_st[:, o + 1:o + 2], l_scr[:, :], AX.X))
        vred = p.seal_vector_inc(sv)
        p.wait("gpsimd", sv, vred)
        p.add("gpsimd", lambda E: E.tensor_mul(
            l_st[:, o + 2:o + 3], l_st[:, o + 0:o + 1], invC[:, :]))
        p.add("gpsimd", lambda E: E.tensor_mul(
            l_st[:, o + 3:o + 4], l_st[:, o + 2:o + 3], l_st[:, o + 2:o + 3]))
        p.add("gpsimd", lambda E: E.tensor_mul(
            l_st[:, o + 4:o + 5], l_st[:, o + 1:o + 2], invC[:, :]))
        p.add("gpsimd", lambda E: E.tensor_sub(
            l_st[:, o + 4:o + 5], l_st[:, o + 4:o + 5], l_st[:, o + 3:o + 4])
            .then_inc(sp.h, 1))
        gva = sp.inc()
        p.wait("scalar", sp, gva)
        p.add("scalar", lambda E: E.activation(
            l_st[:, o + 5:o + 6], l_st[:, o + 4:o + 5], AF.Sqrt, bias=eps_ln[:, 0:1], scale=1.0)
            .then_inc(sa.h, 1))
        saa = sa.inc()
        p.wait("vector", sa, saa)
        vpad(p)
        p.add("vector", lambda E: E.reciprocal(l_st[:, o + 5:o + 6], l_st[:, o + 5:o + 6]))
        vpad(p, 6)
        p.add("vector", lambda E: E.tensor_scalar(
            obuf[:, :], xbuf[:, :], l_st[:, o + 2:o + 3], l_st[:, o + 5:o + 6],
            ALU.subtract, ALU.mult))
        return p.seal_vector_inc(sv)

    qstepB = make_quant_emitter(p1, qbufsA, qtiles[nq_half:])
    ln_xfree, ln_odma = [], []
    ln_srcs = [(x_b, ln1_b, c.TB // 128)] + [(x_o, ln1_o, c.TO // 128)]
    i = 0
    for src_dram, dst_dram, ntiles in ln_srcs:
        for j in range(ntiles):
            par = i % 2
            if i >= 2:
                p1.wait("sync", sv, ln_xfree[i - 2])
            lin = dma(p1, "sync", d[par], l_x[par][:, :], src_dram[j * 128:(j + 1) * 128, :])
            p1.wait("vector", d[par], lin)
            if i >= 2:
                p1.wait("vector", d[2 + par], ln_odma[i - 2])
            ln_xfree.append(ln_tile(p1, l_x[par], l_o[par], par, 1e-5))
            p1.wait("scalar", sv, ln_xfree[i])
            ln_odma.append(dma(p1, "scalar", d[2 + par],
                               dst_dram[j * 128:(j + 1) * 128, :], l_o[par][:, :]))
            qstepB(1)
            i += 1
    qstepB(len(qtiles))
    p1.emit(nc)
    c1.close()
    qctx.close()

    if stages <= 1:
        ctx.close()
        return nc
    kvctx = ExitStack()
    ekv = kvctx.enter_context
    k_fm = ekv(nc.sbuf_tensor("k_fm", [128, c.G * c.TB], BF16))
    v_tm = ekv(nc.sbuf_tensor("v_tm", [128, (c.TB // 128) * c.KV], BF16))
    p2 = Prog()
    phase_start(p2)
    c2 = ExitStack()
    e2 = c2.enter_context
    wk_s = e2(nc.sbuf_tensor("wk_s", [128, c.NCC * c.KV], BF16))
    wv_s = e2(nc.sbuf_tensor("wv_s", [128, c.NCC * c.KV], BF16))

    x_fm = [e2(nc.sbuf_tensor(f"x_fm{i}", [128, c.NCC * 512], BF16, is_transpose_dest=True)) for i in range(2)]
    cs_b = [e2(nc.sbuf_tensor("cs_b0", [128, 1024], F32))] * 2
    k_f32 = e2(nc.sbuf_tensor("k_f32", [128, 512], F32))
    k_sq = e2(nc.sbuf_tensor("k_sq", [128, 512], F32))
    k_row = e2(nc.sbuf_tensor("k_row", [1, 512], F32))
    k_tb = e2(nc.sbuf_tensor("k_tb", [128, 512], F32))

    with nc.psum_tensor("ps_v0", [128, c.KV], F32) as ps_v0, \
         nc.psum_tensor("ps_v1", [128, c.KV], F32) as ps_v1, \
         nc.psum_tensor("ps_k0", [128, 512], F32) as ps_k0, \
         nc.psum_tensor("ps_k1", [128, 512], F32) as ps_k1, \
         nc.psum_tensor("ps_ss2", [1, 512], F32) as ps_ss2, \
         nc.psum_tensor("ps_bc2", [128, 512], F32) as ps_bc2, \
         nc.psum_tensor("ps_sw2", [128, 512], F32) as ps_sw2:
        ps_v = (ps_v0, ps_v1)
        ps_k = (ps_k0, ps_k1)

        for wsrc, wdst in ((wkT, wk_s), (wvT, wv_s)):
            for cc in range(c.NCC):
                kvw = dma(p2, "gpsimd", d[4], wdst[:, cc * c.KV:(cc + 1) * c.KV],
                          wsrc[cc * 128:(cc + 1) * 128, :])

        NT2 = c.TB // 512
        vevac, kdone, xfm_free = [], [], []
        p2.wait("tensor", d[4], kvw)
        for tt in range(NT2):
            par = tt % 2
            if tt >= 2:
                p2.wait("sync", st, xfm_free[tt - 2])
            for cc in range(c.NCC):
                dma(p2, "sync", d[par], x_fm[par][:, cc * 512:(cc + 1) * 512],
                    ln1_b[tt * 512:(tt + 1) * 512, cc * 128:(cc + 1) * 128], transpose=True)
            if tt >= 1:
                p2.wait("sync", sv, kdone[(tt - 1) * c.G + c.G - 1])
            dma(p2, "sync", d[par], cs_b[par][:, 0:512], cos_b[:, tt * 512:(tt + 1) * 512])
            tile_dma = dma(p2, "sync", d[par], cs_b[par][:, 512:1024],
                           sin_b[:, tt * 512:(tt + 1) * 512])

            p2.wait("tensor", d[par], tile_dma)
            p2.wait("vector", d[par], tile_dma)
            for v4 in range(4):
                vp = v4 % 2
                gi = tt * 4 + v4
                if gi >= 2:
                    p2.wait("tensor", sv, vevac[gi - 2])
                for cc in range(c.NCC):
                    p2.add("tensor", lambda E, cc=cc, v4=v4, vp=vp, b=x_fm[par]: E.matmul(
                        ps_v[vp][:, :],
                        b[:, cc * 512 + v4 * 128: cc * 512 + (v4 + 1) * 128],
                        wv_s[:, cc * c.KV:(cc + 1) * c.KV],
                        start=(cc == 0), stop=(cc == c.NCC - 1)))
                vm = p2.seal_tensor_inc(st)
                p2.wait("vector", st, vm)
                p2.add("vector", lambda E, gi=gi, vp=vp: E.tensor_copy(
                    v_tm[:, gi * c.KV:(gi + 1) * c.KV], ps_v[vp][:, :]))
                vevac.append(p2.seal_vector_inc(sv))

            for g in range(c.G):
                kp = g % 2
                gi = tt * c.G + g
                if gi >= 2:
                    p2.wait("tensor", sv, kdone[gi - 2])
                for cc in range(c.NCC):
                    p2.add("tensor", lambda E, cc=cc, g=g, kp=kp, b=x_fm[par]: E.matmul(
                        ps_k[kp][:, :],
                        wk_s[:, cc * c.KV + g * 128: cc * c.KV + (g + 1) * 128],
                        b[:, cc * 512:(cc + 1) * 512],
                        start=(cc == 0), stop=(cc == c.NCC - 1)))
                kmm = p2.seal_tensor_inc(st)
                if g == c.G - 1:
                    xfm_free.append(kmm)

                p2.wait("vector", st, kmm)
                p2.add("vector", lambda E, kp=kp: E.tensor_copy(
                    k_f32[:, :], ps_k[kp][:, :]))
                p2.add("vector", lambda E: E.tensor_mul(k_sq[:, :], k_f32[:, :], k_f32[:, :]))
                ksq_v = p2.seal_vector_inc(sv)
                p2.wait("tensor", sv, ksq_v)
                p2.add("tensor", lambda E: E.matmul(ps_ss2[0:1, :], onesf[:, :], k_sq[:, :],
                                                    start=True, stop=True).then_inc(st.h, 1))
                ssm = st.inc()
                p2.wait("scalar", st, ssm)
                p2.add("scalar", lambda E: E.activation(
                    k_row[0:1, :], ps_ss2[0:1, :], AF.Sqrt, scale=float(1.0 / 128.0))
                    .then_inc(sa.h, 1))
                srt = sa.inc()
                p2.wait("vector", sa, srt)
                p2.add("vector", lambda E: E.tensor_scalar(
                    k_row[0:1, :], k_row[0:1, :], 1e-6, None, ALU.add))
                vpad(p2)
                p2.add("vector", lambda E: E.reciprocal(k_row[0:1, :], k_row[0:1, :]))
                rr = p2.seal_vector_inc(sv)
                p2.wait("tensor", sv, rr)
                p2.add("tensor", lambda E: E.matmul(ps_bc2[:, :], ones1[0:1, :], k_row[0:1, :],
                                                    start=True, stop=True).then_inc(st.h, 1))
                bcm = st.inc()
                p2.wait("vector", st, bcm)
                p2.add("vector", lambda E: E.scalar_tensor_tensor(
                    k_f32[:, :], k_f32[:, :], knw_s[:, 0:1], ps_bc2[:, :], ALU.mult, ALU.mult))
                kn_v = p2.seal_vector_inc(sv)
                p2.wait("tensor", sv, kn_v)
                p2.add("tensor", lambda E: E.matmul(ps_sw2[:, :], swp_s[:, :], k_f32[:, :],
                                                    start=True, stop=True).then_inc(st.h, 1))
                sw_t = st.inc()
                p2.wait("vector", st, sw_t)
                p2.wait("vector", sv, kn_v)
                p2.add("vector", lambda E, b=cs_b[par]: E.tensor_mul(
                    k_f32[:, :], k_f32[:, :], b[:, 0:512]))
                p2.add("vector", lambda E, b=cs_b[par]: E.tensor_mul(
                    k_tb[:, :], ps_sw2[:, :], b[:, 512:1024]))
                p2.add("vector", lambda E, g=g, tt=tt: E.tensor_add(
                    k_fm[0:64, g * c.TB + tt * 512: g * c.TB + (tt + 1) * 512],
                    k_f32[0:64, :], k_tb[0:64, :]))
                p2.add("vector", lambda E, g=g, tt=tt: E.tensor_sub(
                    k_fm[64:128, g * c.TB + tt * 512: g * c.TB + (tt + 1) * 512],
                    k_f32[64:128, :], k_tb[64:128, :]))
                kdone.append(p2.seal_vector_inc(sv))
    p2.emit(nc)
    c2.close()

    if stages <= 2:
        ctx.close()
        return nc
    p3 = Prog()
    phase_start(p3)
    c3 = ExitStack()
    e3 = c3.enter_context
    import os as _os
    QH = int(_os.environ.get('QH', 6))
    q_fm = e3(nc.sbuf_tensor("q_fm", [128, c.H * c.QT], BF16))
    lnofm = e3(nc.sbuf_tensor("lnofm", [128, c.NCC * 512], BF16, is_transpose_dest=True))
    wq_sl = [e3(nc.sbuf_tensor(f"wq_sl{i}", [128, QH * 128], BF16)) for i in range(2)]
    qf32 = e3(nc.sbuf_tensor("qf32", [128, QH * 512], F32))
    q_sq = e3(nc.sbuf_tensor("q_sq", [128, 512], F32))
    q_row = e3(nc.sbuf_tensor("q_row", [1, 512], F32))
    q_ta = e3(nc.sbuf_tensor("q_ta", [128, 512], F32))
    q_tb = e3(nc.sbuf_tensor("q_tb", [128, 512], F32))
    cs_o = e3(nc.sbuf_tensor("cs_o", [128, 1024], F32))
    t_sb = [e3(nc.sbuf_tensor(f"t_sb{i}", [128, c.QT], F32)) for i in range(2)]
    p_sb = [e3(nc.sbuf_tensor(f"p_sb{i}", [128, c.QT], BF16)) for i in range(2)]
    y_f32 = e3(nc.sbuf_tensor("y_f32", [128, c.QT], F32))
    y_out = [e3(nc.sbuf_tensor(f"y_out{i}", [128, c.QT], BF16)) for i in range(2)]
    z_row = e3(nc.sbuf_tensor("z_row", [1, c.QT], F32))

    hchunks = []
    h0 = 0
    while h0 < c.H:
        hchunks.append((h0, min(QH, c.H - h0)))
        h0 += QH

    ydma = []
    yio = []
    wq_guard = {0: 0, 1: 0}
    prev_m_tensor_gate = 0
    prev_m_rope_gate = 0
    lnofm_gate = 0
    for m in range(c.NM):
        p3.wait("sync", st, lnofm_gate)
        p3.wait("sync", sv, prev_m_rope_gate)
        for cc in (range(c.NCC) if "nolno" not in abl else []):
            dma(p3, "sync", d[6], lnofm[:, cc * 512:(cc + 1) * 512],
                ln1_o[m * c.QT:(m + 1) * c.QT, cc * 128:(cc + 1) * 128], transpose=True)
        if "nocs" not in abl:
            dma(p3, "sync", d[6], cs_o[:, 0:512], cos_o[:, m * c.QT:(m + 1) * c.QT])
            lnodma = dma(p3, "sync", d[6], cs_o[:, 512:1024], sin_o[:, m * c.QT:(m + 1) * c.QT])
        else:
            lnodma = 0

        rope_last = 0
        p3.wait("tensor", sv, prev_m_tensor_gate)
        p3.wait("tensor", d[6], lnodma)
        p3.wait("vector", d[6], lnodma)
        for ci, (h0, nh) in enumerate(hchunks if "noqb" not in abl else []):
            with nc.psum_tensor(f"ps_q_{m}_{ci}", [128, QH * 512], F32) as ps_q:
                p3.wait("tensor", sv, rope_last)
                qmm_done = []
                for cc in range(c.NCC):
                    spar = cc % 2
                    p3.wait("sync", st, wq_guard[spar])
                    sl = dma(p3, "sync", d[spar], wq_sl[spar][:, 0:nh * 128],
                             wqT_h[cc * 128:(cc + 1) * 128, h0 * 128:(h0 + nh) * 128])
                    p3.wait("tensor", d[spar], sl)
                    for hh in range(nh):
                        p3.add("tensor", lambda E, cc=cc, hh=hh, spar=spar: E.matmul(
                            ps_q[:, hh * 512:(hh + 1) * 512],
                            wq_sl[spar][:, hh * 128:(hh + 1) * 128],
                            lnofm[:, cc * 512:(cc + 1) * 512],
                            start=(cc == 0), stop=(cc == c.NCC - 1)))
                    qmm_done.append(p3.seal_tensor_inc(st))
                    wq_guard[spar] = qmm_done[-1]
                p3.wait("vector", st, qmm_done[-1])
                for hh in range(nh):
                    p3.add("vector", lambda E, hh=hh: E.tensor_copy(
                        qf32[:, hh * 512:(hh + 1) * 512], ps_q[:, hh * 512:(hh + 1) * 512]))
                qevac = p3.seal_vector_inc(sv)
                if dbg and m == c.NM - 1 and ci == 0:
                    p3.wait("scalar", sv, qevac)
                    dma(p3, "scalar", d[7], q_dbg[:, 0:nh * 512], qf32[:, 0:nh * 512])
            with nc.psum_tensor(f"ps_ss3_{m}_{ci}", [1, 512], F32) as ps_ss3, \
                 nc.psum_tensor(f"ps_bc3_{m}_{ci}", [128, 512], F32) as ps_bc3, \
                 nc.psum_tensor(f"ps_sw3_{m}_{ci}", [128, 512], F32) as ps_sw3:
                p3.wait("tensor", sv, qevac)
                p3.wait("vector", sv, qevac)
                for hh in (range(nh) if "nonorm" not in abl else []):
                    h = h0 + hh
                    p3.add("vector", lambda E, hh=hh: E.tensor_mul(
                        q_sq[:, :], qf32[:, hh * 512:(hh + 1) * 512],
                        qf32[:, hh * 512:(hh + 1) * 512]))
                    qsq_v = p3.seal_vector_inc(sv)
                    p3.wait("tensor", sv, qsq_v)
                    p3.add("tensor", lambda E: E.matmul(
                        ps_ss3[0:1, :], onesf[:, :], q_sq[:, :],
                        start=True, stop=True).then_inc(st.h, 1))
                    ssm = st.inc()
                    p3.wait("scalar", st, ssm)
                    p3.add("scalar", lambda E: E.activation(
                        q_row[0:1, :], ps_ss3[0:1, :], AF.Sqrt, scale=float(1.0 / 128.0))
                        .then_inc(sa.h, 1))
                    p3.wait("vector", sa, sa.inc())
                    p3.add("vector", lambda E: E.tensor_scalar(
                        q_row[0:1, :], q_row[0:1, :], 1e-6, None, ALU.add))
                    vpad(p3)
                    p3.add("vector", lambda E: E.reciprocal(q_row[0:1, :], q_row[0:1, :]))
                    rr = p3.seal_vector_inc(sv)
                    p3.wait("tensor", sv, rr)
                    p3.add("tensor", lambda E: E.matmul(
                        ps_bc3[:, :], ones1[0:1, :], q_row[0:1, :],
                        start=True, stop=True).then_inc(st.h, 1))
                    bcm = st.inc()
                    p3.wait("vector", st, bcm)
                    p3.add("vector", lambda E, hh=hh: E.scalar_tensor_tensor(
                        qf32[:, hh * 512:(hh + 1) * 512], qf32[:, hh * 512:(hh + 1) * 512],
                        qnw_s[:, 0:1], ps_bc3[:, :], ALU.mult, ALU.mult))
                    qn_v = p3.seal_vector_inc(sv)
                    p3.wait("tensor", sv, qn_v)
                    p3.add("tensor", lambda E, hh=hh: E.matmul(
                        ps_sw3[:, :], swp_s[:, :], qf32[:, hh * 512:(hh + 1) * 512],
                        start=True, stop=True).then_inc(st.h, 1))
                    swq_t = st.inc()
                    p3.wait("vector", st, swq_t)
                    p3.wait("vector", sv, qn_v)
                    p3.add("vector", lambda E, hh=hh: E.tensor_mul(
                        q_ta[:, :], qf32[:, hh * 512:(hh + 1) * 512], cs_o[:, 0:512]))
                    p3.add("vector", lambda E: E.tensor_mul(
                        q_tb[:, :], ps_sw3[:, :], cs_o[:, 512:1024]))
                    p3.add("vector", lambda E, h=h: E.tensor_add(
                        q_fm[0:64, h * c.QT:(h + 1) * c.QT], q_ta[0:64, :], q_tb[0:64, :]))
                    p3.add("vector", lambda E, h=h: E.tensor_sub(
                        q_fm[64:128, h * c.QT:(h + 1) * c.QT], q_ta[64:128, :], q_tb[64:128, :]))
                    rope_last = p3.seal_vector_inc(sv)
        prev_m_rope_gate = rope_last
        lnofm_gate_next = None

        NB = (m + 1) * c.SPAN
        with nc.psum_tensor(f"ps_s0_{m}", [128, c.QT], F32) as ps_s0, \
             nc.psum_tensor(f"ps_s1_{m}", [128, c.QT], F32) as ps_s1, \
             nc.psum_tensor(f"ps_y0_{m}", [128, c.QT], F32) as ps_y0, \
             nc.psum_tensor(f"ps_y1_{m}", [128, c.QT], F32) as ps_y1, \
             nc.psum_tensor(f"ps_z0_{m}", [1, c.QT], F32) as ps_z0, \
             nc.psum_tensor(f"ps_z1_{m}", [1, c.QT], F32) as ps_z1, \
             nc.psum_tensor(f"ps_yb0_{m}", [128, c.QT], F32) as ps_yb0, \
             nc.psum_tensor(f"ps_yb1_{m}", [128, c.QT], F32) as ps_yb1:
            ps_s = (ps_s0, ps_s1)
            ps_y = (ps_y0, ps_y1)
            ps_z = (ps_z0, ps_z1)
            ps_yb = (ps_yb0, ps_yb1)
            tanh_done, pvz_done = {}, {}
            m_yio = []
            hlist = list(range(c.H))
            for h in (hlist if "noattn" not in abl else []):
                g = h // (c.H // c.G)
                hp = h % 2
                qk_v = {}
                tanh_v = {}
                mask_v = {}
                exp_v = {}

                def lo_of(b):
                    return max(0, b - m * c.SPAN) * c.CW

                def emit_qk(b):
                    par = b % 2
                    lo = lo_of(b)
                    if (h, b - 2) in tanh_done:
                        p3.wait("tensor", sa, tanh_done[(h, b - 2)])
                    elif b < 2 and (h - 1, NB - 2 + b) in tanh_done:
                        p3.wait("tensor", sa, tanh_done[(h - 1, NB - 2 + b)])
                    if b == 0:
                        p3.wait("tensor", sv, rope_last)
                        if len(m_yio) >= 2:
                            p3.wait("tensor", sv, m_yio[-2])
                    p3.add("tensor", lambda E, g=g, h=h, b=b, par=par, lo=lo: E.matmul(
                        ps_s[par][:, lo:c.QT],
                        k_fm[:, g * c.TB + b * 128: g * c.TB + (b + 1) * 128],
                        q_fm[:, h * c.QT + lo:(h + 1) * c.QT],
                        start=True, stop=True).then_inc(st.h, 1))
                    qk_v[b] = st.inc()

                def emit_tanh(b):
                    par = b % 2
                    lo = lo_of(b)
                    k = b - m * c.SPAN
                    p3.wait("scalar", st, qk_v[b])
                    p3.add("scalar", lambda E, par=par, lo=lo: E.activation(
                        t_sb[par][:, lo:c.QT], ps_s[par][:, lo:c.QT],
                        AF.Tanh, scale=float(1.0 / (50.0 * math.sqrt(128.0))))
                        .then_inc(sa.h, 1))
                    tanh_done[(h, b)] = tanh_v[b] = sa.inc()
                    if k >= 0:
                        p3.wait("vector", sa, tanh_v[b])
                        p3.add("vector", lambda E, par=par, lo=lo: E.tensor_tensor(
                            t_sb[par][:, lo:lo + c.CW], t_sb[par][:, lo:lo + c.CW],
                            mask_s[:, :], ALU.min))
                        mask_v[b] = p3.seal_vector_inc(sv)

                def emit_exp(b):
                    par = b % 2
                    lo = lo_of(b)
                    if b in mask_v:
                        p3.wait("scalar", sv, mask_v[b])
                    if (h, b - 2) in pvz_done:
                        p3.wait("scalar", st, pvz_done[(h, b - 2)])
                    elif b < 2 and (h - 1, NB - 2 + b) in pvz_done:
                        p3.wait("scalar", st, pvz_done[(h - 1, NB - 2 + b)])
                    p3.add("scalar", lambda E, par=par, lo=lo: E.activation(
                        p_sb[par][:, lo:c.QT], t_sb[par][:, lo:c.QT], AF.Exp, scale=50.0)
                        .then_inc(sa.h, 1))
                    exp_v[b] = sa.inc()

                def emit_pvz(b):
                    par = b % 2
                    lo = lo_of(b)
                    p3.wait("tensor", sa, exp_v[b])
                    p3.add("tensor", lambda E, g=g, b=b, hp=hp, par=par, lo=lo, NB=NB: E.matmul(
                        ps_y[hp][:, lo:c.QT],
                        v_tm[:, b * c.KV + g * 128: b * c.KV + (g + 1) * 128],
                        p_sb[par][:, lo:c.QT],
                        start=(b == 0), stop=(b == NB - 1)))
                    p3.add("tensor", lambda E, b=b, hp=hp, par=par, lo=lo, NB=NB: E.matmul(
                        ps_z[hp][0:1, lo:c.QT],
                        ones_b[:, :], p_sb[par][:, lo:c.QT],
                        start=(b == 0), stop=(b == NB - 1)))
                    pvz_done[(h, b)] = p3.seal_tensor_inc(st)

                emit_qk(0)
                if NB > 1:
                    emit_qk(1)
                emit_tanh(0)
                for b in range(NB):
                    if b + 1 < NB:
                        emit_tanh(b + 1)
                    emit_exp(b)
                    if b + 2 < NB:
                        emit_qk(b + 2)
                    emit_pvz(b)
                p3.wait("vector", st, pvz_done[(h, NB - 1)])
                if len(ydma) >= 2:
                    p3.wait("vector", d[2 + (h % 2)], ydma[-2])
                p3.add("vector", lambda E, hp=hp: E.tensor_copy(
                    y_f32[:, :], ps_y[hp][:, :]))
                p3.add("vector", lambda E, hp=hp: E.reciprocal(
                    z_row[0:1, :], ps_z[hp][0:1, :]))
                zr = p3.seal_vector_inc(sv)
                p3.wait("tensor", sv, zr)
                p3.add("tensor", lambda E, hp=hp: E.matmul(
                    ps_yb[hp][:, :], ones1[0:1, :], z_row[0:1, :],
                    start=True, stop=True).then_inc(st.h, 1))
                ybm = st.inc()
                p3.wait("vector", st, ybm)
                p3.add("vector", lambda E, hp=hp: E.tensor_mul(
                    y_out[hp][:, :], y_f32[:, :], ps_yb[hp][:, :]))
                yv = p3.seal_vector_inc(sv)
                yio.append(yv)
                m_yio.append(yv)
                p3.wait("scalar", sv, yv)
                ydma.append(dma(p3, "scalar", d[2 + hp],
                                y_sc[:, h * c.TO + m * c.QT: h * c.TO + (m + 1) * c.QT],
                                y_out[hp][:, :]))
            prev_m_tensor_gate = m_yio[-1] if m_yio else prev_m_tensor_gate
            lnofm_gate = pvz_done[(hlist[-1], NB - 1)] if pvz_done else lnofm_gate
    p3.emit(nc)
    c3.close()
    kvctx.close()

    if stages <= 3:
        ctx.close()
        return nc
    p4 = Prog()
    phase_start(p4)
    c4 = ExitStack()
    e4 = c4.enter_context
    lnb["scr"] = e4(nc.sbuf_tensor("l_scr4", [128, c.C], F32))
    lnb["st"] = e4(nc.sbuf_tensor("l_st4", [128, 12], F32))
    wo_sb = e4(nc.sbuf_tensor("wo_sb", [128, c.H * c.C], BF16))
    bo_bc = e4(nc.sbuf_tensor("bo_bc", [128, c.C], F32))
    bo_row = e4(nc.sbuf_tensor("bo_row", [1, c.C], F32))
    bo_tmp = e4(nc.sbuf_tensor("bo_tmp", [1, 512], F32))
    ysb = [e4(nc.sbuf_tensor(f"ysb{i}", [128, c.H * 128], BF16)) for i in range(2)]
    x0_sb = [e4(nc.sbuf_tensor(f"x0_sb{i}", [128, c.C], F32)) for i in range(2)]
    x1_sb = [e4(nc.sbuf_tensor(f"x1_sb{i}", [128, c.C], F32)) for i in range(2)]
    l2_sb = [e4(nc.sbuf_tensor(f"l2_sb{i}", [128, c.C], BF16)) for i in range(2)]

    with nc.psum_tensor("ps_o0", [128, 512], F32) as ps_o0, \
         nc.psum_tensor("ps_o1", [128, 512], F32) as ps_o1, \
         nc.psum_tensor("ps_bb", [128, 512], F32) as ps_bb:
        ps_o = (ps_o0, ps_o1)
        for h in (range(c.H) if "nowold" not in abl and "nowo" not in abl else []):
            dma(p4, "sync", d[6], wo_sb[:, h * c.C:(h + 1) * c.C], woT_h[h * 128:(h + 1) * 128, :])
        wo_dma = dma(p4, "sync", d[6], bo_row[:, :], bo_t[:, :])
        p4.wait("vector", d[6], wo_dma)
        bo_done_v = 0
        for co in (range(c.NCO) if "nobo" not in abl and "nowo" not in abl else []):
            p4.add("vector", lambda E, co=co: E.tensor_copy(
                bo_tmp[0:1, :], bo_row[0:1, co * 512:(co + 1) * 512]))
            bcp = p4.seal_vector_inc(sv)
            p4.wait("tensor", sv, bcp)
            p4.add("tensor", lambda E: E.matmul(
                ps_bb[:, :], ones1[0:1, :], bo_tmp[0:1, :],
                start=True, stop=True).then_inc(st.h, 1))
            bm = st.inc()
            p4.wait("vector", st, bm)
            p4.add("vector", lambda E, co=co: E.tensor_copy(
                bo_bc[:, co * 512:(co + 1) * 512], ps_bb[:, :]))
            bo_done_v = p4.seal_vector_inc(sv)

        ev_flat = []
        tile_done = []
        x1dma = []
        for tt in (range(c.NTT) if "noloop" not in abl else []):
            par = tt % 2
            if tt >= 2:
                p4.wait("sync", sv, tile_done[tt - 2])
                p4.wait("vector", d[2 + par], x1dma[tt - 2])
            ysrc = y_sc[:, :].rearrange("p (h t) -> p h t", t=c.TO)[:, :, tt * 128:(tt + 1) * 128]
            for h0 in range(0, c.H, 4):
                dma(p4, "sync", d[par],
                    ysb[par][:, h0 * 128:(h0 + 4) * 128].rearrange("p (h t) -> p h t", t=128),
                    ysrc[:, h0:h0 + 4, :])
            ytile_dma = dma(p4, "sync", d[par], x0_sb[par][:, :], x_o[tt * 128:(tt + 1) * 128, :])

            p4.wait("vector", d[par], ytile_dma)
            p4.wait("vector", sv, bo_done_v)
            p4.add("vector", lambda E, b=x0_sb[par]: E.tensor_add(b[:, :], b[:, :], bo_bc[:, :]))
            x0b_v = p4.seal_vector_inc(sv)

            p4.wait("tensor", d[par], ytile_dma)
            for co in (range(c.NCO) if "nooproj" not in abl else []):
                op = (tt * c.NCO + co) % 2
                gi = tt * c.NCO + co
                if gi >= 2:
                    p4.wait("tensor", sv, ev_flat[gi - 2])
                for h in range(c.H):
                    p4.add("tensor", lambda E, h=h, co=co, op=op, b=ysb[par]: E.matmul(
                        ps_o[op][:, :],
                        b[:, h * 128:(h + 1) * 128],
                        wo_sb[:, h * c.C + co * 512: h * c.C + (co + 1) * 512],
                        start=(h == 0), stop=(h == c.H - 1)))
                omm = p4.seal_tensor_inc(st)
                p4.wait("vector", st, omm)
                p4.wait("vector", sv, x0b_v)
                p4.add("vector", lambda E, co=co, op=op, xb=x0_sb[par], xo=x1_sb[par]:
                       E.tensor_add(xo[:, co * 512:(co + 1) * 512],
                                    xb[:, co * 512:(co + 1) * 512],
                                    ps_o[op][:, :]))
                ev_flat.append(p4.seal_vector_inc(sv))
            if "nooproj" in abl:
                p4.add("vector", lambda E, xb=x0_sb[par], xo=x1_sb[par]: E.tensor_copy(xo[:, :], xb[:, :]))
                ev_flat.append(p4.seal_vector_inc(sv))
            if "noln2" not in abl:
                p4.wait("vector", sv, ev_flat[-1] if ev_flat else 0)
                ln2v = ln_tile(p4, x1_sb[par], l2_sb[par], par, 1e-5)
            else:
                p4.add("vector", lambda E, b=l2_sb[par], xb=x1_sb[par]: E.tensor_copy(b[:, :], xb[:, 0:c.C]))
                ln2v = p4.seal_vector_inc(sv)
            tile_done.append(ln2v)
            p4.wait("scalar", sv, ln2v)
            dma(p4, "scalar", d[2 + par], x1_o[tt * 128:(tt + 1) * 128, :], x1_sb[par][:, :])
            x1dma.append(dma(p4, "scalar", d[2 + par], ln2_o[tt * 128:(tt + 1) * 128, :], l2_sb[par][:, :]))
    p4.emit(nc)
    c4.close()

    if stages <= 4:
        ctx.close()
        return nc
    p5 = Prog()
    phase_start(p5)
    c5 = ExitStack()
    e5 = c5.enter_context
    l2fm = e5(nc.sbuf_tensor("l2fm", [128, c.NCC * c.TO], BF16, is_transpose_dest=True))
    g_sl = [e5(nc.sbuf_tensor(f"g_sl{i}", [128, c.NCC * 512], BF16, is_transpose_dest=True)) for i in range(2)]
    u_sl = [e5(nc.sbuf_tensor(f"u_sl{i}", [128, c.NCC * 512], BF16, is_transpose_dest=True)) for i in range(2)]
    sg_t = [e5(nc.sbuf_tensor(f"sg_t{i}", [128, 512], F32)) for i in range(2)]
    h_t = [e5(nc.sbuf_tensor(f"h_t{i}", [128, 512], BF16)) for i in range(2)]
    h_sq = [e5(nc.sbuf_tensor(f"h_sq{i}", [128, 512], F32)) for i in range(2)]
    s2row = e5(nc.sbuf_tensor("s2row", [1, c.TO], F32))
    NQ4 = c.TO // 512

    with nc.psum_tensor("ps_g0", [128, 512], F32) as ps_g0, \
         nc.psum_tensor("ps_g1", [128, 512], F32) as ps_g1, \
         nc.psum_tensor("ps_u0", [128, 512], F32) as ps_u0, \
         nc.psum_tensor("ps_u1", [128, 512], F32) as ps_u1, \
         nc.psum_tensor("ps_s2", [1, NQ4 * 512], F32) as ps_s2:
        ps_g = (ps_g0, ps_g1)
        ps_u = (ps_u0, ps_u1)
        for cc in range(c.NCC):
            l2_dma = dma(p5, "sync", d[6], l2fm[:, cc * c.TO:(cc + 1) * c.TO],
                         ln2_o[:, cc * 128:(cc + 1) * 128], transpose=True)

        qbufsB = make_quant_bufs(e5, "qB")
        qstepC = make_quant_emitter(p5, qbufsB, wptiles)
        h_done, hdma, s2mm_l = [], [], []
        s2q = []
        s2mm_by_gi = {}
        gu_mm_last = {}
        s2mm = 0
        for iu4 in range(c.NU // 4):
            par = iu4 % 2
            if iu4 >= 2:
                p5.wait("sync", st, gu_mm_last[iu4 - 2])
            slab_last = 0
            for base, buf in ((iu4 * 4 * 128, g_sl[par]), (c.U + iu4 * 4 * 128, u_sl[par])):
                for cc in range(c.NCC):
                    slab_last = dma(p5, "sync", d[par], buf[:, cc * 512:(cc + 1) * 512],
                                    wgu_q[base:base + 512, cc * 128:(cc + 1) * 128],
                                    transpose=True)
            p5.wait("tensor", d[par], slab_last)
            if iu4 == 0:
                p5.wait("tensor", d[6], l2_dma)
            for sub in range(4):
                iu = iu4 * 4 + sub
                for tq in range(NQ4):
                    gi = iu * NQ4 + tq
                    gp = gi % 2
                    if gi >= 2:
                        p5.wait("tensor", sv, h_done[gi - 2])
                    for cc in range(c.NCC):
                        p5.add("tensor", lambda E, cc=cc, sub=sub, tq=tq, gp=gp, b=g_sl[par]:
                               E.matmul(ps_g[gp][:, :],
                                        b[:, cc * 512 + sub * 128: cc * 512 + (sub + 1) * 128],
                                        l2fm[:, cc * c.TO + tq * 512: cc * c.TO + (tq + 1) * 512],
                                        start=(cc == 0), stop=(cc == c.NCC - 1)))
                    for cc in range(c.NCC):
                        p5.add("tensor", lambda E, cc=cc, sub=sub, tq=tq, gp=gp, b=u_sl[par]:
                               E.matmul(ps_u[gp][:, :],
                                        b[:, cc * 512 + sub * 128: cc * 512 + (sub + 1) * 128],
                                        l2fm[:, cc * c.TO + tq * 512: cc * c.TO + (tq + 1) * 512],
                                        start=(cc == 0), stop=(cc == c.NCC - 1)))
                    gu = p5.seal_tensor_inc(st)
                    gu_mm_last[iu4] = gu
                    p5.wait("scalar", st, gu)
                    if gi >= 2:
                        p5.wait("scalar", sv, h_done[gi - 2])
                    p5.add("scalar", lambda E, gp=gp, sg=sg_t[gi % 2]: E.activation(
                        sg[:, :], ps_g[gp][:, :], AF.Silu)
                        .then_inc(sa.h, 1))
                    silu_v = sa.inc()
                    p5.wait("vector", sa, silu_v)
                    if gi >= 2:
                        p5.wait("vector", d[2 + (gi % 2)], hdma[gi - 2])
                        if (gi - 2) in s2mm_by_gi:
                            p5.wait("vector", st, s2mm_by_gi[gi - 2])
                    hb = h_t[gi % 2]
                    p5.add("vector", lambda E, gp=gp, hb=hb, sg=sg_t[gi % 2]: E.tensor_mul(
                        hb[:, :], sg[:, :], ps_u[gp][:, :]))
                    p5.add("vector", lambda E, hb=hb, hq=h_sq[gi % 2]: E.tensor_mul(
                        hq[:, :], hb[:, :], hb[:, :]))
                    h_done.append(p5.seal_vector_inc(sv))
                    s2q.append((gi, tq, iu))
                    if len(s2q) >= 2:
                        g2, t2_, i2 = s2q.pop(0)
                        p5.wait("tensor", sv, h_done[g2])
                        p5.add("tensor", lambda E, tq=t2_, iu=i2, hq=h_sq[g2 % 2]: E.matmul(
                            ps_s2[0:1, tq * 512:(tq + 1) * 512], onesf[:, :], hq[:, :],
                            start=(iu == 0), stop=(iu == c.NU - 1)).then_inc(st.h, 1))
                        s2mm = st.inc()
                        s2mm_l.append(s2mm)
                        s2mm_by_gi[g2] = s2mm
                    p5.wait("scalar", sv, h_done[-1])
                    hdma.append(dma(p5, "scalar", d[2 + (gi % 2)],
                                    h_sc[iu * 128:(iu + 1) * 128, tq * 512:(tq + 1) * 512],
                                    hb[:, :]))
                    if gi % 4 == 3:
                        qstepC(1)
        qstepC(len(wptiles))
        while s2q:
            g2, t2_, i2 = s2q.pop(0)
            p5.wait("tensor", sv, h_done[g2])
            p5.add("tensor", lambda E, tq=t2_, iu=i2, hq=h_sq[g2 % 2]: E.matmul(
                ps_s2[0:1, tq * 512:(tq + 1) * 512], onesf[:, :], hq[:, :],
                start=(iu == 0), stop=(iu == c.NU - 1)).then_inc(st.h, 1))
            s2mm = st.inc()
            s2mm_l.append(s2mm)
            s2mm_by_gi[g2] = s2mm
        p5.wait("scalar", st, s2mm)
        p5.add("scalar", lambda E: E.activation(
            s2row[0:1, :], ps_s2[0:1, 0:c.TO], AF.Sqrt, bias=eps_ffn[0:1, 0:1],
            scale=float(1.0 / c.U)).then_inc(sa.h, 1))
        p5.wait("vector", sa, sa.inc())
        vpad(p5)
        p5.add("vector", lambda E: E.reciprocal(s2row[0:1, :], s2row[0:1, :]))
        iv = p5.seal_vector_inc(sv)
        p5.wait("scalar", sv, iv)
        dma(p5, "scalar", d[7], inv_sc[:, :], s2row[0:1, :])
    p5.emit(nc)
    c5.close()

    if stages <= 5:
        ctx.close()
        return nc
    p6 = Prog()
    phase_start(p6)
    c6 = ExitStack()
    e6 = c6.enter_context
    wp_sl = e6(nc.sbuf_tensor("wp_sl", [128, c.NU * 512], BF16, is_transpose_dest=True))
    h_sl = [e6(nc.sbuf_tensor(f"h_sl{i}", [128, c.NU * 128], BF16)) for i in range(2)]
    invcol = e6(nc.sbuf_tensor("invcol", [128, c.NTT], F32))
    x1r = [e6(nc.sbuf_tensor(f"x1r{i}", [128, 512], F32)) for i in range(2)]
    o_sb = [e6(nc.sbuf_tensor(f"o_sb{i}", [128, 512], F32)) for i in range(2)]

    with nc.psum_tensor("ps_f0", [128, 512], F32) as ps_f0, \
         nc.psum_tensor("ps_f1", [128, 512], F32) as ps_f1:
        ps_f = (ps_f0, ps_f1)
        for tt in range(c.NTT):
            inv_dma = dma(p6, "sync", d[6], invcol[:, tt:tt + 1],
                          inv_sc[0:1, tt * 128:(tt + 1) * 128].rearrange("o p -> p o"))
        fmm_done, fev_done, odma = [], [], []
        co_mm_last = 0
        for co in range(c.NCO):
            if co >= 1:
                p6.wait("sync", st, co_mm_last)
            for iu in range(c.NU):
                wp_dma = dma(p6, "sync", d[7], wp_sl[:, iu * 512:(iu + 1) * 512],
                             wp_q[co * 512:(co + 1) * 512, iu * 128:(iu + 1) * 128],
                             transpose=True)
            for tt in range(c.NTT):
                par = tt % 2
                gi = co * c.NTT + tt
                if gi >= 2:
                    p6.wait("sync", st, fmm_done[gi - 2])
                hsrc = h_sc[:, tt * 128:(tt + 1) * 128].rearrange(
                    "(a p) t -> p a t", p=128)
                for a0 in range(0, c.NU, 8):
                    dma(p6, "sync", d[par],
                        h_sl[par][:, a0 * 128:(a0 + 8) * 128].rearrange(
                            "p (a t) -> p a t", t=128),
                        hsrc[:, a0:a0 + 8, :])
                hx_dma = dma(p6, "sync", d[par], x1r[par][:, :],
                             x1_o[tt * 128:(tt + 1) * 128, co * 512:(co + 1) * 512])

                p6.wait("tensor", d[par], hx_dma)
                p6.wait("tensor", d[7], wp_dma)
                if gi >= 2:
                    p6.wait("tensor", sv, fev_done[gi - 2])
                fp = gi % 2
                for iu in range(c.NU):
                    p6.add("tensor", lambda E, iu=iu, fp=fp, b=h_sl[par]: E.matmul(
                        ps_f[fp][:, :],
                        b[:, iu * 128:(iu + 1) * 128],
                        wp_sl[:, iu * 512:(iu + 1) * 512],
                        start=(iu == 0), stop=(iu == c.NU - 1)))
                fmm_done.append(p6.seal_tensor_inc(st))
                p6.wait("vector", st, fmm_done[-1])
                p6.wait("vector", d[par], hx_dma)
                p6.wait("vector", d[6], inv_dma)
                if gi >= 2:
                    p6.wait("vector", d[2 + par], odma[gi - 2])
                p6.add("vector", lambda E, tt=tt, fp=fp, par=par: E.scalar_tensor_tensor(
                    o_sb[par][:, :], ps_f[fp][:, :], invcol[:, tt:tt + 1],
                    x1r[par][:, :], ALU.mult, ALU.add))
                fev_done.append(p6.seal_vector_inc(sv))
                p6.wait("scalar", sv, fev_done[-1])
                odma.append(dma(p6, "scalar", d[2 + par],
                                out_t[tt * 128:(tt + 1) * 128, co * 512:(co + 1) * 512],
                                o_sb[par][:, :]))
            co_mm_last = fmm_done[-1]
        for eng in ENGS:
            p6.wait(eng, d[2], d[2].n)
            p6.wait(eng, d[3], d[3].n)
    p6.emit(nc)
    c6.close()

    ctx.close()
    return nc


def _rope_tables(T, D=128):
    ar = np.arange(0, D, 2, dtype=np.float32)
    inv_freq = (np.float32(1.0) /
                np.power(np.float32(10000.0), ar / np.float32(D))).astype(np.float32)
    t = np.arange(T, dtype=np.float32)
    freqs = np.outer(t, inv_freq).astype(np.float32)
    return np.cos(freqs).astype(np.float32), np.sin(freqs).astype(np.float32)


def make_inputs(cfg, core, x, Wq, Wk, Wv, Wo, bo, qnw, knw, Wgu, Wp):
    c = cfg
    b, p = core // 4, core % 4
    cosT, sinT = _rope_tables(c.TB, 128)
    cosD = np.concatenate([cosT, cosT], axis=1)
    sinD = np.concatenate([sinT, sinT], axis=1)
    jj = np.arange(128)[:, None]
    ii = np.arange(c.CW)[None, :]
    mask = np.where(p + c.S * ii >= jj, np.float32(1e30), np.float32(-1.0)).astype(np.float32)
    swp = np.zeros((128, 128), np.float32)
    for d in range(64):
        swp[64 + d, d] = 1.0
        swp[d, 64 + d] = 1.0
    return {
        "x_b": np.ascontiguousarray(x[b]),
        "x_o": np.ascontiguousarray(x[b, p::c.S]),
        "wqT": np.ascontiguousarray(Wq.T),
        "wkT": np.ascontiguousarray(Wk.T),
        "wvT": np.ascontiguousarray(Wv.T),
        "woT": np.ascontiguousarray(Wo.T),
        "bo": np.ascontiguousarray(bo.reshape(1, -1)),
        "qnw": np.ascontiguousarray(qnw.reshape(128, 1)),
        "knw": np.ascontiguousarray(knw.reshape(128, 1)),
        "cos_b": np.ascontiguousarray(cosD.T),
        "sin_b": np.ascontiguousarray(sinD.T),
        "cos_o": np.ascontiguousarray(cosD[p::c.S].T),
        "sin_o": np.ascontiguousarray(sinD[p::c.S].T),
        "mask": mask,
        "swp": swp,
        "wgu": np.ascontiguousarray(Wgu),
        "wp": np.ascontiguousarray(Wp),
    }


def kernel(L0_attn_x, L0_attn_Wq, L0_attn_Wk, L0_attn_Wv, L0_attn_Wo, L0_attn_bo,
           L0_attn_q_norm_w, L0_attn_k_norm_w, L0_ffn_w_gate_up, L0_ffn_w_proj):
    cfg = Cfg()
    x = np.asarray(L0_attn_x, dtype=np.float32)
    nc = build_nc(cfg)
    in_maps = [make_inputs(cfg, core, x, np.asarray(L0_attn_Wq), np.asarray(L0_attn_Wk),
                           np.asarray(L0_attn_Wv), np.asarray(L0_attn_Wo),
                           np.asarray(L0_attn_bo), np.asarray(L0_attn_q_norm_w),
                           np.asarray(L0_attn_k_norm_w), np.asarray(L0_ffn_w_gate_up),
                           np.asarray(L0_ffn_w_proj))
               for core in range(8)]
    res = run_bass_kernel_spmd(nc, in_maps, core_ids=list(range(8)))
    out = np.empty((2, cfg.TB, cfg.C), dtype=np.float32)
    for core in range(8):
        b, p = core // 4, core % 4
        out[b, p::cfg.S, :] = res.results[core]["out"]
    return out
